# version 09
# baseline (speedup 1.0000x reference)
import numpy as np
import ml_dtypes
from contextlib import ExitStack

import concourse.bass as bass
import concourse.mybir as mybir
from concourse.bass_utils import run_bass_kernel_spmd

AF = mybir.ActivationFunctionType
ALU = mybir.AluOpType
F32 = mybir.dt.float32
BF16 = mybir.dt.bfloat16

D = 1024; H = 16; DK = 64; M = 128; DH = 4096
TPC = 2048; TT = 1024; NT = 2
NFC = 8; NHC = 32
LNEPS = 1e-5; PEPS = 1e-6; AEPS = 1e-6
EXP_BIAS = -0.5 * float(np.log(M))
INV_D = 1.0 / D
C16 = 0.5 / float(np.sqrt(DK))
SCL_PROJ = float(DK) ** -0.25

_DEBUG = ()

TAP_SHAPES = {
    "xn0": (128, 8192), "q0": (128, 8192), "k0": (128, 8192),
    "vc0": (128, 8320), "qf0": (128, 4096), "kf0": (128, 4096),
    "kt0": (128, 4096), "Spfx": (128, 1040), "ao0": (128, 4096),
    "ap0": (128, 4096), "x2_0": (128, 8192), "S_t0": (128, 1040),
    "h0": (128, 32768), "g10": (128, 32768), "g20": (128, 32768),
    "y0": (128, 8192), "af0": (128, 4096), "xn2_0": (128, 8192),
    "pkt0": (128, 4096), "pvc0": (128, 8320), "pxn0": (128, 8192),
}

ENGINES = ("tensor", "scalar", "vector", "gpsimd", "sync")


class _Buf:
    __slots__ = ("writer", "readers")
    def __init__(self):
        self.writer = None; self.readers = {}


class _Op:
    __slots__ = ("eng", "fn", "idx", "deps", "dma_sem", "need_inc", "inc_val",
                 "tag")
    def __init__(self, eng, fn, idx, dma_sem, tag):
        self.eng = eng; self.fn = fn; self.idx = idx
        self.deps = []; self.dma_sem = dma_sem
        self.need_inc = False; self.inc_val = None; self.tag = tag


class Sched:
    def __init__(self):
        self.ops = []; self.bufs = {}; self.dma_count = {}

    def buf(self, name):
        if name not in self.bufs:
            self.bufs[name] = _Buf()
        return self.bufs[name]

    def op(self, eng, fn, reads=(), writes=(), dma_sem=None, tag=""):
        o = _Op(eng, fn, len(self.ops), dma_sem, tag)
        self.ops.append(o)
        ekey = ("dma", dma_sem, eng) if dma_sem is not None else eng
        if dma_sem is not None:
            self.dma_count[dma_sem] = self.dma_count.get(dma_sem, 0) + 1
        deps = set()
        for r in reads:
            b = self.buf(r)
            if b.writer is not None:
                w = self.ops[b.writer]
                if w.idx != o.idx and (w.eng != eng or w.dma_sem is not None
                                       or dma_sem is not None):
                    deps.add(b.writer)
            if b.readers.get(ekey, -1) < o.idx:
                b.readers[ekey] = o.idx
        for wn in writes:
            b = self.buf(wn)
            if b.writer is not None:
                w = self.ops[b.writer]
                if w.idx != o.idx and (w.eng != eng or w.dma_sem is not None
                                       or dma_sem is not None):
                    deps.add(b.writer)
            for rk, ridx in b.readers.items():
                if ridx != o.idx and rk != ekey:
                    deps.add(ridx)
            b.writer = o.idx
            b.readers = {ekey: o.idx} if wn in reads else {}
        o.deps = sorted(deps)
        return o.idx

    _claim_n = 0

    def claim_many(self, news, olds):
        deps = set()
        for on in olds:
            if on in self.bufs:
                b = self.bufs[on]
                if b.writer is not None:
                    deps.add(b.writer)
                deps.update(b.readers.values())
                b.writer = None; b.readers = {}
        for nn in news:
            b = self.buf(nn)
            for d in sorted(deps):
                Sched._claim_n += 1
                b.readers[("claim", Sched._claim_n)] = d

    def emit(self, nc, es):
        for o in self.ops:
            for d in o.deps:
                assert d < o.idx
                self.ops[d].need_inc = True
        esem = {e: es.enter_context(nc.semaphore(f"se_{e}")) for e in ENGINES}
        dsem = {s: es.enter_context(nc.semaphore(f"sd_{s}"))
                for s in self.dma_count}
        cnt = {e: 0 for e in ENGINES}; dcnt = {s: 0 for s in self.dma_count}
        for o in self.ops:
            if o.dma_sem is not None:
                dcnt[o.dma_sem] += 1
                o.inc_val = 16 * dcnt[o.dma_sem]
                o.need_inc = True
            elif o.need_inc:
                cnt[o.eng] += 1
                o.inc_val = cnt[o.eng]
        for o in self.ops:
            for d in o.deps:
                p = self.ops[d]
                if p.dma_sem is not None:
                    for q in self.ops[d + 1:o.idx]:
                        assert q.dma_sem != p.dma_sem, (
                            f"op{o.idx}({o.tag}) waits dma {d}({p.tag}) on "
                            f"{p.dma_sem}, superseded by {q.idx}({q.tag})")
        per_eng = {e: [] for e in ENGINES}
        for o in self.ops:
            per_eng[o.eng].append(o)
        opsref = self.ops
        blk = es.enter_context(nc.Block())
        for ename in ENGINES:
            olist = per_eng[ename]
            if not olist:
                continue
            section = getattr(blk, ename)

            def body(e, olist=olist):
                wm = {}
                for o in olist:
                    need = {}
                    for d in o.deps:
                        p = opsref[d]
                        key = ("d", p.dma_sem) if p.dma_sem is not None \
                            else ("e", p.eng)
                        if p.inc_val > need.get(key, -1):
                            need[key] = p.inc_val
                    for key, v in sorted(need.items()):
                        if v > wm.get(key, 0):
                            e.wait_ge(dsem[key[1]] if key[0] == "d"
                                      else esem[key[1]], v)
                            wm[key] = v
                    inst = o.fn(e)
                    if o.need_inc:
                        assert inst is not None, o.tag
                        if o.dma_sem is not None:
                            inst.then_inc(dsem[o.dma_sem], 16)
                        else:
                            inst.then_inc(esem[o.eng], 1)
            section(body)


def _blocked(w):
    O, I = w.shape
    n_oc, n_ic = O // 128, I // 128
    ocs = 32 // n_ic
    ns = n_oc // ocs
    v = w.reshape(ns, ocs, 128, n_ic, 128).transpose(0, 4, 1, 3, 2)
    return np.ascontiguousarray(v.reshape(ns, 128, 4096))


def _pcol(v):
    return np.ascontiguousarray(np.asarray(v, np.float32).reshape(-1, 128).T)


def build_program(debug=()):
    nc = bass.Bass("TRN2", target_bir_lowering=False)

    def din(name, shape, dtype=F32):
        return nc.dram_tensor(name, shape, dtype, kind="ExternalInput").ap()

    xo = din("xo", [D, TPC]); xp = din("xp", [D, TPC])
    flag = din("flag", [128, 1])
    wdr = {n: din(n, [s, 128, 4096]) for n, s in
           (("wq", 2), ("wk", 2), ("wv", 2), ("wo", 2), ("w1", 8), ("w2", 8),
            ("h1", 32), ("h2", 32), ("w3", 8))}
    proj2 = din("proj2", [128, 128])
    identc = din("identc", [128, 128])
    maskc = din("maskc", [128, 128])
    selc = din("selc", [2, 256])
    c16b = din("c16b", [128, 2])
    onesn = din("onesn", [128, 128])
    ones1 = din("ones1", [1, 128])
    bqc = din("bqc", [128, 8]); bkc = din("bkc", [128, 8]); boc = din("boc", [128, 8])
    bvrc = din("bvrc", [1, 1024])
    s1c = din("s1c", [128, 32]); s2c = din("s2c", [128, 32])
    hs1c = din("hs1c", [128, 32]); hs2c = din("hs2c", [128, 32])
    ws3c = din("ws3c", [128, 8])
    cstc = din("cstc", [128, 2])
    out = nc.dram_tensor("out", [D, TPC], F32, kind="ExternalOutput").ap()
    dbg = {}
    for name in debug:
        dbg[name] = nc.dram_tensor("dbg_" + name, list(TAP_SHAPES[name]), F32,
                                   kind="ExternalOutput").ap()

    es = ExitStack()
    sb = lambda n, s, d: es.enter_context(nc.sbuf_tensor(n, s, d))
    psum = lambda n, s, d: es.enter_context(nc.psum_tensor(n, s, d))

    xres = sb("xres", [128, NFC * TT], F32)
    wsl = [sb(f"wsl{i}", [128, 4096], BF16) for i in range(4)]
    S_ext = sb("S_ext", [128, H * 65], F32)
    pj2s = sb("pj2s", [128, 128], BF16)
    idn = sb("idn", [128, 128], BF16)
    msk = sb("msk", [128, 128], BF16)
    sel = sb("sel", [2, 256], BF16)
    c16p = sb("c16p", [128, 2], BF16)
    onn = sb("onn", [128, 128], BF16)
    on1 = sb("on1", [1, 128], BF16)
    bq_s = sb("bq_s", [128, 8], F32); bk_s = sb("bk_s", [128, 8], F32)
    bo_s = sb("bo_s", [128, 8], F32)
    s1_s = sb("s1_s", [128, 32], F32); s2_s = sb("s2_s", [128, 32], F32)
    hs1_s = sb("hs1_s", [128, 32], F32); hs2_s = sb("hs2_s", [128, 32], F32)
    ws3_s = sb("ws3_s", [128, 8], F32)
    flg = sb("flg", [128, 1], F32)
    cst = sb("cst", [128, 2], F32)
    pj2f = sb("pj2f", [128, 128], F32)
    mB = sb("mB", [128, 512], F32)
    vm = sb("vm", [128, 512], F32)
    t32 = sb("t32", [128, 512], F32)
    t32g = sb("t32g", [128, 512], F32)
    den = sb("den", [128, 16], F32)
    rec = sb("rec", [128, 16], F32)

    AR = 65536
    arena = sb("arena", [128, AR], BF16)
    OFF = {}
    def carve(name, n, off):
        OFF[name] = (off, n)
        assert off + n <= AR
        return off + n

    o = 0
    o = carve("xn", NFC * TT, o)
    o = carve("QK", 2 * NFC * TT, o)
    o = carve("vc", 8 * H * 65, o)
    o = carve("qkf", 2 * H * 256, o)
    o = carve("kt", 2 * H * 128, o)
    o = carve("ao", 4 * 1024, o)
    o = carve("af", NFC * 512, o)
    o = carve("ap_", NFC * 512, o)
    o = carve("sqs", 2048, o)
    o = carve("sq_sb", 2048, o)
    o = carve("S_bf", H * 65, o)
    o = carve("Asb", 256, o)
    o = carve("bv_s", 1024, o)
    assert o <= 65536 - 1648, f"arena used {o}"
    OFF["xb"] = (OFF["qkf"][0], 2 * NFC * 512)
    OFF["hb"] = (32768, 32768)
    OFF["g1"] = (0, 32768)
    OFF["yb"] = OFF["xn"]

    def A(name, p0=0, p1=128):
        off, n = OFF[name]
        return arena[p0:p1, off:off + n]

    def Asl(name, c0, c1, p0=0, p1=128):
        off, n = OFF[name]
        assert c1 <= n
        return arena[p0:p1, off + c0:off + c1]

    pb = [psum(f"pb{i}", [128, 512], F32) for i in range(7)]
    pT = psum("pT", [128, 1024], BF16)

    sch = Sched()
    A_KEYS = ["Q", "K", "QK", "qkf", "vc", "kt", "ao", "af", "ap_", "sqs",
              "sq_sb", "S_bf", "Asb", "bv_s", "xb0", "xb1", "xn"]

    def gdma(dst, src, sem, reads=(), writes=(), tag=""):
        return sch.op("gpsimd", lambda e, d=dst, s=src: e.dma_start(d, s),
                      reads=reads, writes=writes, dma_sem=sem, tag=tag)

    def sdma(dst, src, sem, reads=(), writes=(), tag=""):
        return sch.op("sync", lambda e, d=dst, s=src: e.dma_start(d, s),
                      reads=reads, writes=writes, dma_sem=sem, tag=tag)

    _pool = [3, 4, 5, 6]
    _pool_i = [0]
    def pool():
        b = _pool[_pool_i[0] % 4]
        _pool_i[0] += 1
        return b

    class WStream:
        def __init__(self):
            self.items = []; self.emitted = 0
        def push(self, src, tag):
            self.items.append((src, tag))
            return len(self.items) - 1
        def prime(self, n=4):
            while self.emitted < min(n, len(self.items)):
                self._load(self.emitted)
        def _load(self, i):
            src, tag = self.items[i]
            slot = i % 4
            gdma(wsl[slot][:, :], src, f"dw{slot}", writes=[f"wsl{slot}"],
                 tag=f"ld_{tag}")
            self.emitted += 1
        def done_upto(self, i):
            while self.emitted < min(i + 5, len(self.items)):
                self._load(self.emitted)
        def slot(self, i):
            assert i < self.emitted
            return i % 4
    ws = WStream()

    def tile_ap(slab_i, j):
        s = ws.slot(slab_i)
        return wsl[s][:, j * 128:(j + 1) * 128], f"wsl{s}"

    gdma(idn[:, :], identc[:, :], "c0", writes=["idn"])
    gdma(msk[:, :], maskc[:, :], "c1", writes=["msk"])
    gdma(sel[:, :], selc[:, :], "c2", writes=["sel"])
    gdma(c16p[:, :], c16b[:, :], "c3", writes=["c16p"])
    gdma(onn[:, :], onesn[:, :], "c4", writes=["onn"])
    gdma(on1[:, :], ones1[:, :], "c5", writes=["on1"])
    sdma(bq_s[:, :], bqc[:, :], "c6", writes=["bq_s"])
    sdma(bk_s[:, :], bkc[:, :], "c7", writes=["bk_s"])
    sdma(bo_s[:, :], boc[:, :], "c8", writes=["bo_s"])
    sdma(s1_s[:, :], s1c[:, :], "c9", writes=["s1_s"])
    sdma(s2_s[:, :], s2c[:, :], "c10", writes=["s2_s"])
    sdma(hs1_s[:, :], hs1c[:, :], "c11", writes=["hs1_s"])
    sdma(hs2_s[:, :], hs2c[:, :], "c12", writes=["hs2_s"])
    sdma(ws3_s[:, :], ws3c[:, :], "c13", writes=["ws3_s"])
    sdma(flg[:, :], flag[:, :], "c14", writes=["flg"])
    sdma(cst[:, :], cstc[:, :], "c16", writes=["cst"])
    sdma(pj2f[:, :], proj2[:, :], "c15", writes=["pj2f"])
    sch.op("scalar", lambda e: e.activation(pj2s[:, :], pj2f[:, :], AF.Copy,
                                            scale=SCL_PROJ),
           reads=["pj2f"], writes=["pj2s"], tag="pjs")
    sch.op("vector", lambda e: e.memset(S_ext[:, :], 0.0), writes=["S"],
           tag="S0")

    def set_vc_ones():
        v3 = A("vc").rearrange("p (ch e) -> p ch e", e=65)
        sch.op("vector", lambda e, v3=v3: e.memset(v3[:, :, 64:65], 1.0),
               reads=["vc"], writes=["vc"], tag="vc1")

    def load_bv():
        gdma(Asl("bv_s", 0, 1024, 0, 1), bvrc[:, :], "dbv",
             reads=["bv_s"], writes=["bv_s"], tag="bv")

    def tap(name, src_ap, reads):
        if name not in dbg:
            return
        eng = gdma if str(src_ap.dtype) != "dt.float32" else sdma
        eng(dbg[name][:, :], src_ap, f"tp_{name}",
            reads=list(reads), writes=["tapdram"], tag=f"tap_{name}")
    sch.buf("tapdram")

    def ln_group(srcf, skey, xresf, dstf, dkey, mode, tagp, psplit=False,
                 xkey="xres"):
        b1, b2 = pool(), pool()
        for fc in range(NFC):
            sch.op("tensor",
                   lambda e, fc=fc: e.matmul(pb[b1][:, :], onn[:, :], srcf(fc),
                                             start=(fc == 0),
                                             stop=(fc == NFC - 1)),
                   reads=["onn", skey], writes=[f"pb{b1}"], tag=f"{tagp}s1")
        for half in range(2):
            for i in range(4):
                fc = half * 4 + i
                sch.op("vector",
                       lambda e, fc=fc, i=i: e.tensor_mul(
                           Asl("sqs", i * 512, (i + 1) * 512), srcf(fc),
                           srcf(fc)),
                       reads=[skey], writes=["sqs"], tag=f"{tagp}sq")
            for i in range(4):
                fc = half * 4 + i
                sch.op("tensor",
                       lambda e, fc=fc, i=i: e.matmul(
                           pb[b2][:, :], onn[:, :],
                           Asl("sqs", i * 512, (i + 1) * 512),
                           start=(fc == 0), stop=(fc == NFC - 1)),
                       reads=["onn", "sqs"], writes=[f"pb{b2}"],
                       tag=f"{tagp}s2")
        sch.op("vector", lambda e: e.tensor_copy(mB[:, :], pb[b1][:, :]),
               reads=[f"pb{b1}"], writes=["mB"], tag=f"{tagp}mB")
        sch.op("vector", lambda e: e.tensor_mul(vm[:, :], mB[:, :], mB[:, :]),
               reads=["mB"], writes=["vm"], tag=f"{tagp}m2")
        sch.op("vector", lambda e: e.scalar_tensor_tensor(
            vm[:, :], pb[b2][:, :], 1.0, vm[:, :], ALU.mult, ALU.subtract),
               reads=[f"pb{b2}", "vm"], writes=["vm"], tag=f"{tagp}var")
        sch.op("scalar", lambda e: e.activation(vm[:, :], vm[:, :], AF.Ln,
                                                bias=cst[:, 0:1]),
               reads=["vm", "cst"], writes=["vm"], tag=f"{tagp}ln")
        sch.op("scalar", lambda e: e.activation(vm[:, :], vm[:, :], AF.Exp,
                                                scale=-0.5),
               reads=["vm"], writes=["vm"], tag=f"{tagp}ex")
        for fc in range(NFC):
            png = False
            eng = "gpsimd" if png else "vector"
            tsl = (lambda: t32g[:, :]) if png else (lambda: t32[:, :])
            tkey = "t32g" if png else "t32"
            if mode == "xn":
                sch.op(eng,
                       lambda e, fc=fc, tsl=tsl: e.tensor_sub(
                           tsl(), xresf(fc), mB[:, :]),
                       reads=[xkey, "mB"], writes=[tkey], tag=f"{tagp}a1")
                sch.op(eng,
                       lambda e, fc=fc, tsl=tsl: e.tensor_mul(
                           dstf(fc), tsl(), vm[:, :]),
                       reads=[tkey, "vm"], writes=[dkey], tag=f"{tagp}a2")
            else:
                sch.op(eng,
                       lambda e, fc=fc, tsl=tsl: e.tensor_sub(
                           tsl(), srcf(fc), mB[:, :]),
                       reads=[skey, "mB"], writes=[tkey], tag=f"{tagp}a1")
                sch.op(eng,
                       lambda e, fc=fc, tsl=tsl: e.tensor_mul(
                           tsl(), tsl(), vm[:, :]),
                       reads=[tkey, "vm"], writes=[tkey], tag=f"{tagp}a2")
                sch.op(eng,
                       lambda e, fc=fc, tsl=tsl: e.tensor_add(
                           xresf(fc), xresf(fc), tsl()),
                       reads=[tkey, xkey], writes=[xkey],
                       tag=f"{tagp}a3")

    def ln_full(srcf, skey, dstf, dkey, mode, tagp):
        for jg in range(2):
            ln_group(lambda fc, jg=jg: srcf(fc, jg), skey,
                     (lambda fc, jg=jg: xres[:, fc * TT + jg * 512:
                                             fc * TT + jg * 512 + 512]),
                     (lambda fc, jg=jg: dstf(fc, jg)) if dstf else None,
                     dkey, mode, f"{tagp}{jg}")

    def proj_qk(dkey, slab0, bias_sb, bkey, tagp, jgs=(0, 1)):
        for jg in jgs:
            for oc in range(NFC):
                si = slab0 + oc // 4
                bnk = pool()
                for ic in range(NFC):
                    j = (oc % 4) * 8 + ic
                    w_ap, wkey = tile_ap(si, j)
                    sch.op("tensor",
                           lambda e, w_ap=w_ap, ic=ic, jg=jg, bnk=bnk:
                           e.matmul(pb[bnk][:, :], w_ap,
                                    Asl("xn", ic * TT + jg * 512,
                                        ic * TT + jg * 512 + 512),
                                    start=(ic == 0), stop=(ic == NFC - 1)),
                           reads=[wkey, "xn"], writes=[f"pb{bnk}"],
                           tag=f"{tagp}mm")
                qk = 0 if dkey == "Q" else 1024
                sch.op("scalar",
                       lambda e, oc=oc, jg=jg, bnk=bnk, qk=qk: e.activation(
                           Asl("QK", oc * 2048 + qk + jg * 512,
                               oc * 2048 + qk + jg * 512 + 512),
                           pb[bnk][:, :], AF.Identity,
                           bias=bias_sb[:, oc:oc + 1]),
                       reads=[f"pb{bnk}", bkey], writes=[dkey],
                       tag=f"{tagp}ev")

    def v_upfront(slab0, tagp):
        for c in range(8):
            for og in range(2):
                si = slab0 + og
                bnk = pool()
                s = ws.slot(si)
                rhs = wsl[s].rearrange("p (o4 ic o) -> p o4 ic o", o4=4, ic=8)
                for ic in range(NFC):
                    sch.op("tensor",
                           lambda e, c=c, ic=ic, rhs=rhs, bnk=bnk: e.matmul(
                               pb[bnk][:, :],
                               Asl("xn", ic * TT + c * 128,
                                   ic * TT + (c + 1) * 128),
                               rhs[:, 0:4, ic, :],
                               start=(ic == 0), stop=False),
                           reads=[f"wsl{s}", "xn"], writes=[f"pb{bnk}"],
                           tag=f"{tagp}mm")
                sch.op("tensor",
                       lambda e, og=og, bnk=bnk: e.matmul(
                           pb[bnk][:, :], on1[0:1, :],
                           Asl("bv_s", og * 512, (og + 1) * 512, 0, 1),
                           start=False, stop=True),
                       reads=["on1", "bv_s"], writes=[f"pb{bnk}"],
                       tag=f"{tagp}b")
                v3 = A("vc").rearrange("p (ch e) -> p ch e", e=65)
                sch.op("scalar",
                       lambda e, c=c, og=og, bnk=bnk, v3=v3: e.activation(
                           v3[:, c * H + og * 8:c * H + (og + 1) * 8, 0:64],
                           pb[bnk][:, :], AF.Copy),
                       reads=[f"pb{bnk}"], writes=["vc"], tag=f"{tagp}ev")

    def phi_group(t0, with_q, tagp):
        def qk_ap(fc, r0, r1, qk, c0, c1):
            return Asl("QK", fc * 2048 + qk * 1024 + t0 + c0,
                       fc * 2048 + qk * 1024 + t0 + c1, r0, r1)

        def emit_sq(hp):
            so = (hp % 4) * 512
            if with_q:
                sch.op("vector",
                       lambda e, hp=hp, so=so: e.tensor_mul(
                           Asl("sqs", so, so + 256),
                           qk_ap(hp, 0, 128, 0, 0, 256),
                           qk_ap(hp, 0, 128, 0, 0, 256)),
                       reads=["Q"], writes=["sqs"], tag=f"{tagp}q2")
            sch.op("vector",
                   lambda e, hp=hp, so=so: e.tensor_mul(
                       Asl("sqs", so + 256, so + 512),
                       qk_ap(hp, 0, 128, 1, 0, 256),
                       qk_ap(hp, 0, 128, 1, 0, 256)),
                   reads=["K"], writes=["sqs"], tag=f"{tagp}k2")

        def emit_sqmm(hp):
            so = (hp % 4) * 512
            bnk = pool()
            sqc0 = 0 if with_q else 256
            sch.op("tensor",
                   lambda e, bnk=bnk, sqc0=sqc0, so=so: e.matmul(
                       pb[bnk][0:2, sqc0:512], c16p[:, :],
                       Asl("sqs", so + sqc0, so + 512),
                       start=True, stop=True),
                   reads=["c16p", "sqs"], writes=[f"pb{bnk}"],
                   tag=f"{tagp}sqm")
            sch.op("vector",
                   lambda e, bnk=bnk, sqc0=sqc0, so=so: e.tensor_copy(
                       Asl("sq_sb", so + sqc0, so + 512, 0, 2),
                       pb[bnk][0:2, sqc0:512]),
                   reads=[f"pb{bnk}"], writes=["sq_sb"], tag=f"{tagp}sqe")

        def emit_u(hp):
            bo = (hp % 4) * 512
            for hh in range(2):
                h = 2 * hp + hh
                base = 64 * hh
                selsl = sel[:, hh * 128:(hh + 1) * 128]
                if with_q:
                    rhs = A("QK").rearrange("p (fc a t) -> p fc a t",
                                            a=2, t=TT)[
                        base:base + 64, hp, :, t0:t0 + 256]
                    bnk = pool()
                    sch.op("tensor",
                           lambda e, rhs=rhs, base=base, bnk=bnk: e.matmul(
                               pb[bnk][:, 0:512],
                               pj2s[base:base + 64, :], rhs,
                               start=True, stop=False,
                               tile_position=(base, 0)),
                           reads=["pj2s", "Q", "K"], writes=[f"pb{bnk}"],
                           tag=f"{tagp}qku")
                    sch.op("tensor",
                           lambda e, selsl=selsl, bnk=bnk, bo=bo: e.matmul(
                               pb[bnk][:, 0:512], selsl,
                               Asl("sq_sb", bo, bo + 512, 0, 2),
                               start=False, stop=True),
                           reads=["sel", "sq_sb"], writes=[f"pb{bnk}"],
                           tag=f"{tagp}qkn")
                    sch.op("scalar",
                           lambda e, h=h, bnk=bnk: e.activation(
                               Asl("qkf", h * 512, (h + 1) * 512),
                               pb[bnk][:, 0:512], AF.Exp, bias=cst[:, 1:2]),
                           reads=[f"pb{bnk}", "cst"], writes=["qkf"],
                           tag=f"{tagp}qke")
                bnk = pool()
                for cc in range(2):
                    sch.op("tensor",
                           lambda e, hp=hp, base=base, cc=cc, bnk=bnk:
                           e.matmul(pb[bnk][:, cc * 128:(cc + 1) * 128],
                                    qk_ap(hp, base, base + 64, 1, cc * 128,
                                          (cc + 1) * 128),
                                    pj2s[base:base + 64, :],
                                    start=True, stop=False,
                                    tile_position=(base, 0)),
                           reads=["K", "pj2s"], writes=[f"pb{bnk}"],
                           tag=f"{tagp}tu")
                    sch.op("tensor",
                           lambda e, cc=cc, selsl=selsl, bnk=bnk, bo=bo:
                           e.matmul(pb[bnk][:, cc * 128:(cc + 1) * 128],
                                    Asl("sq_sb", bo + 256 + cc * 128,
                                        bo + 256 + (cc + 1) * 128, 0, 2),
                                    selsl, start=False, stop=True),
                           reads=["sq_sb", "sel"], writes=[f"pb{bnk}"],
                           tag=f"{tagp}tn")
                sch.op("scalar",
                       lambda e, h=h, bnk=bnk: e.activation(
                           Asl("kt", h * 256, (h + 1) * 256),
                           pb[bnk][:, 0:256], AF.Exp, bias=cst[:, 1:2]),
                       reads=[f"pb{bnk}", "cst"], writes=["kt"],
                       tag=f"{tagp}te")

        for step in range(NFC + 2):
            if step < NFC:
                emit_sq(step)
            if step >= 1 and step - 1 < NFC:
                emit_sqmm(step - 1)
            if step >= 2:
                emit_u(step - 2)
        if with_q:
            sch.op("vector",
                   lambda e: e.tensor_scalar_add(A("qkf"), A("qkf"), PEPS),
                   reads=["qkf"], writes=["qkf"], tag=f"{tagp}ep")
        sch.op("vector",
               lambda e: e.tensor_scalar_add(A("kt"), A("kt"), PEPS),
               reads=["kt"], writes=["kt"], tag=f"{tagp}ept")

    def s_bank(h):
        if h < 7: return 0, h * 65
        if h < 14: return 1, (h - 7) * 65
        return 2, (h - 14) * 65

    sbank_started = {}
    def s_start(bk):
        if sbank_started.get(bk):
            return False
        sbank_started[bk] = True
        return True

    def s_reset():
        sbank_started.clear()

    def s_carry_in(tagp):
        sch.op("vector", lambda e: e.tensor_copy(A("S_bf"), S_ext[:, :]),
               reads=["S"], writes=["S_bf"], tag=f"{tagp}cbf")
        s_reset()
        for h in range(H):
            bk, c0 = s_bank(h)
            st = s_start(bk)
            sch.op("tensor",
                   lambda e, h=h, bk=bk, c0=c0, st=st: e.matmul(
                       pb[bk][:, c0:c0 + 65], idn[:, :],
                       Asl("S_bf", h * 65, (h + 1) * 65),
                       start=st, stop=False),
                   reads=["idn", "S_bf"], writes=[f"pb{bk}"], tag=f"{tagp}ci")

    def s_carry_out(tagp):
        sch.op("vector", lambda e: e.tensor_copy(S_ext[:, 0:455],
                                                 pb[0][:, 0:455]),
               reads=["pb0"], writes=["S"], tag=f"{tagp}co0")
        sch.op("vector", lambda e: e.tensor_copy(S_ext[:, 455:910],
                                                 pb[1][:, 0:455]),
               reads=["pb1"], writes=["S"], tag=f"{tagp}co1")
        sch.op("vector", lambda e: e.tensor_copy(S_ext[:, 910:1040],
                                                 pb[2][:, 0:130]),
               reads=["pb2"], writes=["S"], tag=f"{tagp}co2")

    def s_cast(tagp):
        sch.op("scalar", lambda e: e.activation(Asl("S_bf", 0, 455),
                                                pb[0][:, 0:455], AF.Copy),
               reads=["pb0"], writes=["S_bf"], tag=f"{tagp}sc0")
        sch.op("scalar", lambda e: e.activation(Asl("S_bf", 455, 910),
                                                pb[1][:, 0:455], AF.Copy),
               reads=["pb1"], writes=["S_bf"], tag=f"{tagp}sc1")
        sch.op("scalar", lambda e: e.activation(Asl("S_bf", 910, 1040),
                                                pb[2][:, 0:130], AF.Copy),
               reads=["pb2"], writes=["S_bf"], tag=f"{tagp}sc2")

    def attn_chunk(c, cc, full, tagp):
        def emit_T1(h):
            qsl = Asl("qkf", h * 512 + cc * 128, h * 512 + (cc + 1) * 128)
            ksl = Asl("qkf", h * 512 + 256 + cc * 128,
                      h * 512 + 256 + (cc + 1) * 128)
            bA = 3 + (h % 2)
            sch.op("tensor",
                   lambda e, ksl=ksl, qsl=qsl, bA=bA: e.matmul(
                       pb[bA][:, 0:128], ksl, qsl, start=True, stop=True),
                   reads=["qkf"], writes=[f"pb{bA}"], tag=f"{tagp}T1")

        def emit_dv(h):
            aosl = Asl("ao", (c % 4) * 1024 + h * 64,
                       (c % 4) * 1024 + (h + 1) * 64)
            rk = "recL" if h < 8 else "recH"
            sch.op("scalar",
                   lambda e, aosl=aosl, h=h: e.activation(
                       aosl, aosl, AF.Copy, scale=rec[:, h:h + 1]),
                   reads=["ao", rk], writes=["ao"], tag=f"{tagp}dv")

        def emit_tr(fc):
            sch.op("sync",
                   lambda e, fc=fc: e.dma_start_transpose(
                       Asl("af", fc * 512 + (c % 4) * 128,
                           fc * 512 + (c % 4) * 128 + 128),
                       Asl("ao", (c % 4) * 1024 + fc * 128,
                           (c % 4) * 1024 + (fc + 1) * 128)),
                   reads=["ao"], writes=["af"], dma_sem="dtr",
                   tag=f"{tagp}tr")

        if full:
            emit_T1(0)
        for h in range(H):
            qsl = Asl("qkf", h * 512 + cc * 128, h * 512 + (cc + 1) * 128)
            ktsl = Asl("kt", h * 256 + cc * 128, h * 256 + (cc + 1) * 128)
            vsl = Asl("vc", (c * H + h) * 65, (c * H + h + 1) * 65)
            bk, c0 = s_bank(h)
            if full:
                if h + 1 < H:
                    emit_T1(h + 1)
                bA = 3 + (h % 2)
                asb = Asl("Asb", (h % 2) * 128, (h % 2) * 128 + 128)
                sch.op("vector",
                       lambda e, asb=asb, bA=bA: e.tensor_mul(
                           asb, pb[bA][:, 0:128], msk[:, :]),
                       reads=[f"pb{bA}", "msk"], writes=["Asb"],
                       tag=f"{tagp}mk")
            st = s_start(bk)
            sch.op("tensor",
                   lambda e, ktsl=ktsl, vsl=vsl, bk=bk, c0=c0, st=st:
                   e.matmul(pb[bk][:, c0:c0 + 65], ktsl, vsl,
                            start=st, stop=False),
                   reads=["kt", "vc"], writes=[f"pb{bk}"], tag=f"{tagp}T4")
            if full:
                bN = 5 + (h % 2)
                sch.op("tensor",
                       lambda e, asb=asb, vsl=vsl, bN=bN: e.matmul(
                           pb[bN][:, 0:65], asb, vsl, start=True, stop=False),
                       reads=["Asb", "vc"], writes=[f"pb{bN}"],
                       tag=f"{tagp}T2")
                sch.op("tensor",
                       lambda e, qsl=qsl, h=h, bN=bN: e.matmul(
                           pb[bN][:, 0:65], qsl,
                           Asl("S_bf", h * 65, (h + 1) * 65),
                           start=False, stop=True),
                       reads=["qkf", "S_bf"], writes=[f"pb{bN}"],
                       tag=f"{tagp}T3")
                dcol = den[:, h:h + 1]
                sch.op("vector",
                       lambda e, dcol=dcol, bN=bN: e.tensor_scalar_add(
                           dcol, pb[bN][:, 64:65], AEPS),
                       reads=[f"pb{bN}"], writes=["den"], tag=f"{tagp}d1")
                aosl = Asl("ao", (c % 4) * 1024 + h * 64,
                           (c % 4) * 1024 + (h + 1) * 64)
                sch.op("vector",
                       lambda e, aosl=aosl, bN=bN: e.tensor_copy(
                           aosl, pb[bN][:, 0:64]),
                       reads=[f"pb{bN}"], writes=["ao"], tag=f"{tagp}nc")
                if h == 7:
                    sch.op("vector",
                           lambda e: e.reciprocal(rec[:, 0:8], den[:, 0:8]),
                           reads=["den"], writes=["recL"], tag=f"{tagp}rcL")
                if h >= 8:
                    emit_dv(h - 8)
                    if h % 2 == 1:
                        emit_tr((h - 9) // 2)
        if full:
            sch.op("vector",
                   lambda e: e.reciprocal(rec[:, 8:16], den[:, 8:16]),
                   reads=["den"], writes=["recH"], tag=f"{tagp}rcH")
            s_cast(tagp)
            for j in range(4):
                emit_dv(8 + 2 * j)
                emit_dv(9 + 2 * j)
                emit_tr(4 + j)

    def wo_proj(g, slab0, tagp):
        for oc in range(NFC):
            si = slab0 + oc // 4
            bnk = pool()
            for ic in range(NFC):
                j = (oc % 4) * 8 + ic
                w_ap, wkey = tile_ap(si, j)
                sch.op("tensor",
                       lambda e, w_ap=w_ap, ic=ic, bnk=bnk: e.matmul(
                           pb[bnk][:, :], w_ap,
                           Asl("af", ic * 512, (ic + 1) * 512),
                           start=(ic == 0), stop=(ic == NFC - 1)),
                       reads=[wkey, "af"], writes=[f"pb{bnk}"],
                       tag=f"{tagp}mm")
            sch.op("scalar",
                   lambda e, oc=oc, bnk=bnk: e.activation(
                       Asl("ap_", oc * 512, (oc + 1) * 512), pb[bnk][:, :],
                       AF.Identity, bias=bo_s[:, oc:oc + 1]),
                   reads=[f"pb{bnk}", "bo_s"], writes=["ap_"],
                   tag=f"{tagp}ev")

    def ln3(g, tagp):
        ln_group(lambda fc: Asl("ap_", fc * 512, (fc + 1) * 512), "ap_",
                 lambda fc: xres[:, fc * TT + g * 512:
                                 fc * TT + (g + 1) * 512],
                 None, None, "res", f"{tagp}ln", xkey=f"xres{g}")

    def xsrc(src, t):
        return src.rearrange("(fc p) t -> p fc t", p=128)[
            :, :, t * TT:(t + 1) * TT]

    pfx = [ws.push(wdr["wk"][i], f"pwk{i}") for i in range(2)] + \
          [ws.push(wdr["wv"][i], f"pwv{i}") for i in range(2)]
    tile_slabs = []
    for t in range(NT):
        d = {}
        d["wq"] = [ws.push(wdr["wq"][i], f"t{t}wq{i}") for i in range(2)]
        d["wk"] = [ws.push(wdr["wk"][i], f"t{t}wk{i}") for i in range(2)]
        d["wv"] = [ws.push(wdr["wv"][i], f"t{t}wv{i}") for i in range(2)]
        d["wo"] = [ws.push(wdr["wo"][i], f"t{t}wo{i}") for i in range(2)]
        s1l = []
        for b in range(8):
            s1l.append(ws.push(wdr["w1"][b], f"t{t}w1_{b}"))
            s1l.append(ws.push(wdr["w2"][b], f"t{t}w2_{b}"))
        d["s1"] = s1l
        d["h1"] = [ws.push(wdr["h1"][i], f"t{t}h1_{i}") for i in range(32)]
        d["h2"] = [ws.push(wdr["h2"][i], f"t{t}h2_{i}") for i in range(32)]
        d["w3"] = [ws.push(wdr["w3"][i], f"t{t}w3_{i}") for i in range(8)]
        tile_slabs.append(d)
    ws.prime(4)

    set_vc_ones()
    load_bv()

    s_reset()
    for pt in range(2):
        tg = f"p{pt}"
        for jg in range(2):
            sdma(xres.rearrange("p (fc t) -> p fc t", t=TT)[
                     :, :, jg * 512:(jg + 1) * 512],
                 xsrc(xp, pt)[:, :, jg * 512:(jg + 1) * 512], f"dx{jg}",
                 reads=[f"xres{jg}"], writes=[f"xres{jg}"],
                 tag=f"{tg}x{jg}")
            gdma(A("xb").rearrange("p (fc t) -> p fc t", t=TT)[
                     :, :, jg * 512:(jg + 1) * 512],
                 xsrc(xp, pt)[:, :, jg * 512:(jg + 1) * 512], f"dxb{jg}",
                 reads=[f"xb{jg}"], writes=[f"xb{jg}"], tag=f"{tg}xb{jg}")
        for jg in range(2):
            ln_group(lambda fc, jg=jg: Asl("xb", fc * TT + jg * 512,
                                           fc * TT + jg * 512 + 512),
                     f"xb{jg}",
                     lambda fc, jg=jg: xres[:, fc * TT + jg * 512:
                                            fc * TT + jg * 512 + 512],
                     lambda fc, jg=jg: Asl("xn", fc * TT + jg * 512,
                                           fc * TT + jg * 512 + 512),
                     "xn", "xn", f"{tg}l1_{jg}", psplit=True,
                     xkey=f"xres{jg}")
            proj_qk("K", pfx[0], bk_s, "bk_s", f"{tg}K", jgs=(jg,))
        if pt == 0:
            tap("pxn0", A("xn"), ["xn"])
        if pt == 1:
            ws.done_upto(pfx[1])
        v_upfront(pfx[2], f"{tg}V")
        if pt == 1:
            ws.done_upto(pfx[3])
        if pt == 0:
            tap("pvc0", A("vc"), ["vc"])
        for g in range(4):
            phi_group(g * 256, False, f"{tg}f{g}")
            if pt == 0 and g == 0:
                tap("pkt0", A("kt"), ["kt"])
            for cc in range(2):
                attn_chunk(g * 2 + cc, cc, False, f"{tg}c{g * 2 + cc}")
    s_carry_out("pco")
    sch.op("vector", lambda e: e.tensor_scalar_mul(S_ext[:, :], S_ext[:, :],
                                                   flg[:, :]),
           reads=["S", "flg"], writes=["S"], tag="Sflag")
    tap("Spfx", S_ext[:, :], ["S"])

    for t in range(NT):
        tg = f"t{t}"
        sl = tile_slabs[t]
        sch.claim_many(A_KEYS, ["hb", "g1", "yb"])
        if t > 0:
            set_vc_ones()
            load_bv()
        for jg in range(2):
            sdma(xres.rearrange("p (fc t) -> p fc t", t=TT)[
                     :, :, jg * 512:(jg + 1) * 512],
                 xsrc(xo, t)[:, :, jg * 512:(jg + 1) * 512], f"dx{jg}",
                 reads=[f"xres{jg}"], writes=[f"xres{jg}"],
                 tag=f"{tg}x{jg}")
            gdma(A("xb").rearrange("p (fc t) -> p fc t", t=TT)[
                     :, :, jg * 512:(jg + 1) * 512],
                 xsrc(xo, t)[:, :, jg * 512:(jg + 1) * 512], f"dxb{jg}",
                 reads=[f"xb{jg}"], writes=[f"xb{jg}"], tag=f"{tg}xb{jg}")
        for jg in range(2):
            ln_group(lambda fc, jg=jg: Asl("xb", fc * TT + jg * 512,
                                           fc * TT + jg * 512 + 512),
                     f"xb{jg}",
                     lambda fc, jg=jg: xres[:, fc * TT + jg * 512:
                                            fc * TT + jg * 512 + 512],
                     lambda fc, jg=jg: Asl("xn", fc * TT + jg * 512,
                                           fc * TT + jg * 512 + 512),
                     "xn", "xn", f"{tg}l1_{jg}", psplit=True,
                     xkey=f"xres{jg}")
            proj_qk("Q", sl["wq"][0], bq_s, "bq_s", f"{tg}Q", jgs=(jg,))
        if t == 0:
            tap("xn0", A("xn"), ["xn"])
        ws.done_upto(sl["wq"][1])
        proj_qk("K", sl["wk"][0], bk_s, "bk_s", f"{tg}K")
        ws.done_upto(sl["wk"][1])

        v_upfront(sl["wv"][0], f"{tg}V")
        ws.done_upto(sl["wv"][1])
        if t == 0:
            tap("vc0", A("vc"), ["vc"])
        s_carry_in(f"{tg}ci")
        pend_ln3 = None
        for g in range(4):
            phi_group(g * 256, True, f"{tg}f{g}")
            if pend_ln3 is not None:
                pend_ln3(); pend_ln3 = None
            if t == 0 and g == 0:
                tap("kt0", A("kt"), ["kt"])
            for cc in range(2):
                c = g * 2 + cc
                attn_chunk(c, cc, True, f"{tg}c{c}")
            if g % 2 == 1:
                if t == 0 and g == 1:
                    tap("ao0", A("ao"), ["ao"]); tap("af0", A("af"), ["af"])
                wo_proj(g // 2, sl["wo"][0], f"{tg}wo{g // 2}")
                pend_ln3 = (lambda gg=g // 2: ln3(gg, f"{tg}wo{gg}"))
                if t == 0 and g == 1:
                    tap("ap0", A("ap_"), ["ap_"])
        if pend_ln3 is not None:
            pend_ln3(); pend_ln3 = None
        ws.done_upto(sl["wo"][1])
        s_carry_out(f"{tg}co")
        if t == 0:
            tap("x2_0", xres[:, :], ["xres0", "xres1"])
            tap("S_t0", S_ext[:, :], ["S"])

        sch.claim_many(["hb", "g1"],
                       ["Q", "K", "QK", "qkf", "vc", "kt", "ao", "af", "ap_",
                        "S_bf", "Asb", "bv_s"])
        for jg in range(2):
            for fc in range(NFC):
                sch.op("vector",
                       lambda e, fc=fc, jg=jg: e.tensor_copy(
                           Asl("xb", fc * TT + jg * 512,
                               fc * TT + jg * 512 + 512),
                           xres[:, fc * TT + jg * 512:
                                fc * TT + jg * 512 + 512]),
                       reads=[f"xres{jg}"], writes=[f"xb{jg}"],
                       tag=f"{tg}l2c")
            ln_group(lambda fc, jg=jg: Asl("xb", fc * TT + jg * 512,
                                           fc * TT + jg * 512 + 512),
                     f"xb{jg}",
                     lambda fc, jg=jg: xres[:, fc * TT + jg * 512:
                                            fc * TT + jg * 512 + 512],
                     lambda fc, jg=jg: Asl("xn", fc * TT + jg * 512,
                                           fc * TT + jg * 512 + 512),
                     "xn", "xn", f"{tg}l2_{jg}", psplit=True,
                     xkey=f"xres{jg}")
        if t == 0:
            tap("xn2_0", A("xn"), ["xn"])
        sch.claim_many(["hb"], ["xb0", "xb1", "sqs", "sq_sb"])
        for b in range(8):
            i1, i2 = sl["s1"][2 * b], sl["s1"][2 * b + 1]
            for o4 in range(4):
                oc = b * 4 + o4
                for jg in range(2):
                    b1, b2 = pool(), pool()
                    for ic in range(NFC):
                        j = o4 * 8 + ic
                        w1ap, w1k = tile_ap(i1, j)
                        w2ap, w2k = tile_ap(i2, j)
                        sch.op("tensor",
                               lambda e, w1ap=w1ap, ic=ic, jg=jg, b1=b1:
                               e.matmul(pb[b1][:, :], w1ap,
                                        Asl("xn", ic * TT + jg * 512,
                                            ic * TT + jg * 512 + 512),
                                        start=(ic == 0),
                                        stop=(ic == NFC - 1)),
                               reads=[w1k, "xn"], writes=[f"pb{b1}"],
                               tag=f"{tg}s1a")
                        sch.op("tensor",
                               lambda e, w2ap=w2ap, ic=ic, jg=jg, b2=b2:
                               e.matmul(pb[b2][:, :], w2ap,
                                        Asl("xn", ic * TT + jg * 512,
                                            ic * TT + jg * 512 + 512),
                                        start=(ic == 0),
                                        stop=(ic == NFC - 1)),
                               reads=[w2k, "xn"], writes=[f"pb{b2}"],
                               tag=f"{tg}s1b")
                    sch.op("scalar",
                           lambda e, oc=oc, b1=b1: e.activation(
                               t32[:, :], pb[b1][:, :], AF.Silu,
                               scale=s1_s[:, oc:oc + 1]),
                           reads=[f"pb{b1}", "s1_s"], writes=["t32"],
                           tag=f"{tg}si")
                    sch.op("vector",
                           lambda e, oc=oc, jg=jg, b2=b2:
                           e.scalar_tensor_tensor(
                               Asl("hb", oc * TT + jg * 512,
                                   oc * TT + jg * 512 + 512),
                               pb[b2][:, :], s2_s[:, oc:oc + 1], t32[:, :],
                               ALU.mult, ALU.mult),
                           reads=[f"pb{b2}", "s2_s", "t32"], writes=["hb"],
                           tag=f"{tg}hm")
            ws.done_upto(i2)
        if t == 0:
            tap("h0", A("hb"), ["hb"])
        sch.claim_many(["g1"], ["xn"])

        def hstage(slabs, skey, src_name, dkey, dst_name, scol, sckey, tagp):
            for oc in range(NHC):
                si = slabs[oc]
                for jg in range(2):
                    bnk = pool()
                    for ic in range(NHC):
                        w_ap, wkey = tile_ap(si, ic)
                        sch.op("tensor",
                               lambda e, w_ap=w_ap, ic=ic, jg=jg, bnk=bnk:
                               e.matmul(pb[bnk][:, :], w_ap,
                                        Asl(src_name, ic * TT + jg * 512,
                                            ic * TT + jg * 512 + 512),
                                        start=(ic == 0),
                                        stop=(ic == NHC - 1)),
                               reads=[wkey, skey], writes=[f"pb{bnk}"],
                               tag=f"{tagp}mm")
                    sch.op("scalar",
                           lambda e, oc=oc, jg=jg, bnk=bnk: e.activation(
                               Asl(dst_name, oc * TT + jg * 512,
                                   oc * TT + jg * 512 + 512),
                               pb[bnk][:, :], AF.Gelu,
                               scale=scol[:, oc:oc + 1]),
                           reads=[f"pb{bnk}", sckey], writes=[dkey],
                           tag=f"{tagp}ev")
                ws.done_upto(si)
        hstage(sl["h1"], "hb", "hb", "g1", "g1", hs1_s, "hs1_s", f"{tg}H1")
        if t == 0:
            tap("g10", A("g1"), ["g1"])
        hstage(sl["h2"], "g1", "g1", "hb", "hb", hs2_s, "hs2_s", f"{tg}H2")
        if t == 0:
            tap("g20", A("hb"), ["hb"])
        sch.claim_many(["yb"], ["g1"])
        for oc in range(NFC):
            si = sl["w3"][oc]
            for jg in range(2):
                bnk = pool()
                for ic in range(NHC):
                    w_ap, wkey = tile_ap(si, ic)
                    sch.op("tensor",
                           lambda e, w_ap=w_ap, ic=ic, jg=jg, bnk=bnk:
                           e.matmul(pb[bnk][:, :], w_ap,
                                    Asl("hb", ic * TT + jg * 512,
                                        ic * TT + jg * 512 + 512),
                                    start=(ic == 0), stop=(ic == NHC - 1)),
                           reads=[wkey, "hb"], writes=[f"pb{bnk}"],
                           tag=f"{tg}w3mm")
                sch.op("vector",
                       lambda e, oc=oc, jg=jg, bnk=bnk: e.tensor_scalar_mul(
                           Asl("yb", oc * TT + jg * 512,
                               oc * TT + jg * 512 + 512),
                           pb[bnk][:, :], ws3_s[:, oc:oc + 1]),
                       reads=[f"pb{bnk}", "ws3_s"], writes=["yb"],
                       tag=f"{tg}w3ev")
            ws.done_upto(si)
        if t == 0:
            tap("y0", A("yb"), ["yb"])
        sch.claim_many(["sqs", "hb"], ["hb"])
        for jg in range(2):
            ln_group(lambda fc, jg=jg: Asl("yb", fc * TT + jg * 512,
                                           fc * TT + jg * 512 + 512), "yb",
                     lambda fc, jg=jg: xres[:, fc * TT + jg * 512:
                                            fc * TT + jg * 512 + 512],
                     None, None, "res", f"{tg}l4_{jg}", psplit=True,
                     xkey=f"xres{jg}")
        sdma(out.rearrange("(fc p) t -> p fc t", p=128)[
                 :, :, t * TT:(t + 1) * TT],
             xres.rearrange("p (fc t) -> p fc t", t=TT), "dout",
             reads=["xres0", "xres1"], writes=["outdram"], tag=f"{tg}out")

    sch.op("sync", lambda e: None, reads=["outdram", "tapdram"], tag="fin")
    sch.emit(nc, es)
    es.close()
    return nc


_prog_cache = {}

def _get_program():
    if "p" not in _prog_cache:
        _prog_cache["p"] = build_program(debug=_DEBUG)
    return _prog_cache["p"]


def kernel(L0_attn_x, L0_attn_Wq, L0_attn_bq, L0_attn_Wk, L0_attn_bk,
           L0_attn_Wv, L0_attn_bv, L0_attn_Wo, L0_attn_bo, L0_attn_proj,
           L0_ffn_w1_w, L0_ffn_w1_s, L0_ffn_w2_w, L0_ffn_w2_s,
           L0_ffn_h1_w, L0_ffn_h1_s, L0_ffn_h2_w, L0_ffn_h2_s,
           L0_ffn_w3_w, L0_ffn_w3_s):
    f32 = lambda a: np.ascontiguousarray(np.asarray(a, dtype=np.float32))
    x = f32(L0_attn_x)
    pj = f32(L0_attn_proj)
    selv = np.zeros((2, 256), np.float32)
    selv[0, 0:128] = -1.0; selv[1, 128:256] = -1.0
    c16bv = np.zeros((128, 2), np.float32)
    c16bv[0:64, 0] = C16; c16bv[64:128, 1] = C16
    shared = {
        "wq": _blocked(f32(L0_attn_Wq)), "wk": _blocked(f32(L0_attn_Wk)),
        "wv": _blocked(f32(L0_attn_Wv)), "wo": _blocked(f32(L0_attn_Wo)),
        "w1": _blocked(f32(L0_ffn_w1_w)), "w2": _blocked(f32(L0_ffn_w2_w)),
        "h1": _blocked(f32(L0_ffn_h1_w)), "h2": _blocked(f32(L0_ffn_h2_w)),
        "w3": _blocked(f32(L0_ffn_w3_w)),
        "proj2": np.ascontiguousarray(np.concatenate([pj.T, pj.T], axis=0)),
        "identc": np.eye(128, dtype=np.float32),
        "maskc": np.ascontiguousarray(np.triu(np.ones((128, 128),
                                                      np.float32))),
        "selc": selv, "c16b": c16bv,
        "onesn": np.full((128, 128), INV_D, np.float32),
        "ones1": np.ones((1, 128), np.float32),
        "bqc": _pcol(L0_attn_bq), "bkc": _pcol(L0_attn_bk),
        "boc": _pcol(L0_attn_bo),
        "bvrc": f32(L0_attn_bv).reshape(1, 1024),
        "s1c": _pcol(L0_ffn_w1_s), "s2c": _pcol(L0_ffn_w2_s),
        "hs1c": _pcol(L0_ffn_h1_s), "hs2c": _pcol(L0_ffn_h2_s),
        "ws3c": _pcol(L0_ffn_w3_s),
        "cstc": np.ascontiguousarray(np.broadcast_to(
            np.array([LNEPS, EXP_BIAS], np.float32), (128, 2))),
    }
    in_maps = []
    for c in range(8):
        b, half = c // 2, c % 2
        m = dict(shared)
        m["xo"] = np.ascontiguousarray(x[b, half * TPC:(half + 1) * TPC, :].T)
        if half == 0:
            m["xp"] = np.zeros((D, TPC), np.float32)
            m["flag"] = np.zeros((128, 1), np.float32)
        else:
            m["xp"] = np.ascontiguousarray(x[b, 0:TPC, :].T)
            m["flag"] = np.ones((128, 1), np.float32)
        in_maps.append(m)
    nc = _get_program()
    res = run_bass_kernel_spmd(nc, in_maps, core_ids=list(range(8)))
    globals()["_last_results"] = res.results
    outp = np.empty((4, 4096, 1024), np.float32)
    for c in range(8):
        b, half = c // 2, c % 2
        outp[b, half * TPC:(half + 1) * TPC, :] = res.results[c]["out"].T
    return outp


# version 10
# speedup vs baseline: 1.0442x; 1.0442x over previous
import numpy as np
import ml_dtypes
from contextlib import ExitStack

import concourse.bass as bass
import concourse.mybir as mybir
from concourse.bass_utils import run_bass_kernel_spmd

AF = mybir.ActivationFunctionType
ALU = mybir.AluOpType
F32 = mybir.dt.float32
BF16 = mybir.dt.bfloat16

D = 1024; H = 16; DK = 64; M = 128; DH = 4096
TPC = 2048; TT = 1024; NT = 2
NFC = 8; NHC = 32
LNEPS = 1e-5; PEPS = 1e-6; AEPS = 1e-6
EXP_BIAS = -0.5 * float(np.log(M))
INV_D = 1.0 / D
C16 = 0.5 / float(np.sqrt(DK))
SCL_PROJ = float(DK) ** -0.25

_DEBUG = ()

TAP_SHAPES = {
    "xn0": (128, 8192), "q0": (128, 8192), "k0": (128, 8192),
    "vc0": (128, 8320), "qf0": (128, 4096), "kf0": (128, 4096),
    "kt0": (128, 4096), "Spfx": (128, 1040), "ao0": (128, 4096),
    "ap0": (128, 4096), "x2_0": (128, 8192), "S_t0": (128, 1040),
    "h0": (128, 32768), "g10": (128, 32768), "g20": (128, 32768),
    "y0": (128, 8192), "af0": (128, 4096), "xn2_0": (128, 8192),
    "pkt0": (128, 4096), "pvc0": (128, 8320), "pxn0": (128, 8192),
}

ENGINES = ("tensor", "scalar", "vector", "gpsimd", "sync")


class _Buf:
    __slots__ = ("writer", "readers")
    def __init__(self):
        self.writer = None; self.readers = {}


class _Op:
    __slots__ = ("eng", "fn", "idx", "deps", "dma_sem", "need_inc", "inc_val",
                 "tag")
    def __init__(self, eng, fn, idx, dma_sem, tag):
        self.eng = eng; self.fn = fn; self.idx = idx
        self.deps = []; self.dma_sem = dma_sem
        self.need_inc = False; self.inc_val = None; self.tag = tag


class Sched:
    def __init__(self):
        self.ops = []; self.bufs = {}; self.dma_count = {}

    def buf(self, name):
        if name not in self.bufs:
            self.bufs[name] = _Buf()
        return self.bufs[name]

    def op(self, eng, fn, reads=(), writes=(), dma_sem=None, tag=""):
        o = _Op(eng, fn, len(self.ops), dma_sem, tag)
        self.ops.append(o)
        ekey = ("dma", dma_sem, eng) if dma_sem is not None else eng
        if dma_sem is not None:
            self.dma_count[dma_sem] = self.dma_count.get(dma_sem, 0) + 1
        deps = set()
        for r in reads:
            b = self.buf(r)
            if b.writer is not None:
                w = self.ops[b.writer]
                if w.idx != o.idx and (w.eng != eng or w.dma_sem is not None
                                       or dma_sem is not None):
                    deps.add(b.writer)
            if b.readers.get(ekey, -1) < o.idx:
                b.readers[ekey] = o.idx
        for wn in writes:
            b = self.buf(wn)
            if b.writer is not None:
                w = self.ops[b.writer]
                if w.idx != o.idx and (w.eng != eng or w.dma_sem is not None
                                       or dma_sem is not None):
                    deps.add(b.writer)
            for rk, ridx in b.readers.items():
                if ridx != o.idx and rk != ekey:
                    deps.add(ridx)
            b.writer = o.idx
            b.readers = {ekey: o.idx} if wn in reads else {}
        o.deps = sorted(deps)
        return o.idx

    _claim_n = 0

    def claim_many(self, news, olds):
        deps = set()
        for on in olds:
            if on in self.bufs:
                b = self.bufs[on]
                if b.writer is not None:
                    deps.add(b.writer)
                deps.update(b.readers.values())
                b.writer = None; b.readers = {}
        for nn in news:
            b = self.buf(nn)
            for d in sorted(deps):
                Sched._claim_n += 1
                b.readers[("claim", Sched._claim_n)] = d

    def emit(self, nc, es):
        for o in self.ops:
            for d in o.deps:
                assert d < o.idx
                self.ops[d].need_inc = True
        esem = {e: es.enter_context(nc.semaphore(f"se_{e}")) for e in ENGINES}
        dsem = {s: es.enter_context(nc.semaphore(f"sd_{s}"))
                for s in self.dma_count}
        cnt = {e: 0 for e in ENGINES}; dcnt = {s: 0 for s in self.dma_count}
        for o in self.ops:
            if o.dma_sem is not None:
                dcnt[o.dma_sem] += 1
                o.inc_val = 16 * dcnt[o.dma_sem]
                o.need_inc = True
            elif o.need_inc:
                cnt[o.eng] += 1
                o.inc_val = cnt[o.eng]
        last_by_sem = {}
        latest_before = {}
        running = {}
        for o in self.ops:
            latest_before[o.idx] = dict(running)
            if o.dma_sem is not None:
                running[o.dma_sem] = o.idx
        for o in self.ops:
            nd = set()
            for d in o.deps:
                p = self.ops[d]
                if p.dma_sem is not None:
                    b = latest_before[o.idx].get(p.dma_sem)
                    assert b is not None and b >= d
                    nd.add(b)
                else:
                    nd.add(d)
            o.deps = sorted(nd)
        per_eng = {e: [] for e in ENGINES}
        for o in self.ops:
            per_eng[o.eng].append(o)
        opsref = self.ops
        blk = es.enter_context(nc.Block())
        for ename in ENGINES:
            olist = per_eng[ename]
            if not olist:
                continue
            section = getattr(blk, ename)

            def body(e, olist=olist):
                wm = {}
                for o in olist:
                    need = {}
                    for d in o.deps:
                        p = opsref[d]
                        key = ("d", p.dma_sem) if p.dma_sem is not None \
                            else ("e", p.eng)
                        if p.inc_val > need.get(key, -1):
                            need[key] = p.inc_val
                    for key, v in sorted(need.items()):
                        if v > wm.get(key, 0):
                            e.wait_ge(dsem[key[1]] if key[0] == "d"
                                      else esem[key[1]], v)
                            wm[key] = v
                    inst = o.fn(e)
                    if o.need_inc:
                        assert inst is not None, o.tag
                        if o.dma_sem is not None:
                            inst.then_inc(dsem[o.dma_sem], 16)
                        else:
                            inst.then_inc(esem[o.eng], 1)
            section(body)


def _blocked(w):
    O, I = w.shape
    n_oc, n_ic = O // 128, I // 128
    ocs = 32 // n_ic
    ns = n_oc // ocs
    v = w.reshape(ns, ocs, 128, n_ic, 128).transpose(0, 4, 1, 3, 2)
    return np.ascontiguousarray(v.reshape(ns, 128, 4096))


def _pcol(v):
    return np.ascontiguousarray(np.asarray(v, np.float32).reshape(-1, 128).T)


def build_program(debug=()):
    nc = bass.Bass("TRN2", target_bir_lowering=False)

    def din(name, shape, dtype=F32):
        return nc.dram_tensor(name, shape, dtype, kind="ExternalInput").ap()

    xo = din("xo", [D, TPC]); xp = din("xp", [D, TPC])
    flag = din("flag", [128, 1])
    wdr = {n: din(n, [s, 128, 4096]) for n, s in
           (("wq", 2), ("wk", 2), ("wv", 2), ("wo", 2), ("w1", 8), ("w2", 8),
            ("h1", 32), ("h2", 32), ("w3", 8))}
    proj2 = din("proj2", [128, 128])
    identc = din("identc", [128, 128])
    maskc = din("maskc", [128, 128])
    selc = din("selc", [2, 256])
    c16b = din("c16b", [128, 2])
    onesn = din("onesn", [128, 128])
    ones1 = din("ones1", [1, 128])
    bqc = din("bqc", [128, 8]); bkc = din("bkc", [128, 8]); boc = din("boc", [128, 8])
    bvrc = din("bvrc", [1, 1024])
    s1c = din("s1c", [128, 32]); s2c = din("s2c", [128, 32])
    hs1c = din("hs1c", [128, 32]); hs2c = din("hs2c", [128, 32])
    ws3c = din("ws3c", [128, 8])
    cstc = din("cstc", [128, 2])
    out = nc.dram_tensor("out", [D, TPC], F32, kind="ExternalOutput").ap()
    dbg = {}
    for name in debug:
        dbg[name] = nc.dram_tensor("dbg_" + name, list(TAP_SHAPES[name]), F32,
                                   kind="ExternalOutput").ap()

    es = ExitStack()
    sb = lambda n, s, d: es.enter_context(nc.sbuf_tensor(n, s, d))
    psum = lambda n, s, d: es.enter_context(nc.psum_tensor(n, s, d))

    xres = sb("xres", [128, NFC * TT], F32)
    wsl = [sb(f"wsl{i}", [128, 4096], BF16) for i in range(4)]
    S_ext = sb("S_ext", [128, H * 65], F32)
    pj2s = sb("pj2s", [128, 128], BF16)
    idn = sb("idn", [128, 128], BF16)
    msk = sb("msk", [128, 128], BF16)
    sel = sb("sel", [2, 256], BF16)
    c16p = sb("c16p", [128, 2], BF16)
    onn = sb("onn", [128, 128], BF16)
    on1 = sb("on1", [1, 128], BF16)
    bq_s = sb("bq_s", [128, 8], F32); bk_s = sb("bk_s", [128, 8], F32)
    bo_s = sb("bo_s", [128, 8], F32)
    s1_s = sb("s1_s", [128, 32], F32); s2_s = sb("s2_s", [128, 32], F32)
    hs1_s = sb("hs1_s", [128, 32], F32); hs2_s = sb("hs2_s", [128, 32], F32)
    ws3_s = sb("ws3_s", [128, 8], F32)
    flg = sb("flg", [128, 1], F32)
    cst = sb("cst", [128, 2], F32)
    pj2f = sb("pj2f", [128, 128], F32)
    mB = sb("mB", [128, 512], F32)
    vm = sb("vm", [128, 512], F32)
    t32 = sb("t32", [128, 512], F32)
    t32g = sb("t32g", [128, 512], F32)
    den = sb("den", [128, 16], F32)
    rec = sb("rec", [128, 16], F32)

    AR = 65536
    arena = sb("arena", [128, AR], BF16)
    OFF = {}
    def carve(name, n, off):
        OFF[name] = (off, n)
        assert off + n <= AR
        return off + n

    o = 0
    o = carve("xn", NFC * TT, o)
    o = carve("QK", 2 * NFC * TT, o)
    o = carve("vc", 8 * H * 65, o)
    o = carve("qkf", 2 * H * 256, o)
    o = carve("kt", 2 * H * 128, o)
    o = carve("ao", 4 * 1024, o)
    o = carve("af", NFC * 512, o)
    o = carve("ap_", NFC * 512, o)
    o = carve("sqs", 2048, o)
    o = carve("sq_sb", 2048, o)
    o = carve("S_bf", H * 65, o)
    o = carve("Asb", 256, o)
    o = carve("bv_s", 1024, o)
    assert o <= 65536 - 1648, f"arena used {o}"
    OFF["xb"] = (OFF["qkf"][0], 2 * NFC * 512)
    OFF["hb"] = (32768, 32768)
    OFF["g1"] = (0, 32768)
    OFF["yb"] = OFF["xn"]

    def A(name, p0=0, p1=128):
        off, n = OFF[name]
        return arena[p0:p1, off:off + n]

    def Asl(name, c0, c1, p0=0, p1=128):
        off, n = OFF[name]
        assert c1 <= n
        return arena[p0:p1, off + c0:off + c1]

    pb = [psum(f"pb{i}", [128, 512], F32) for i in range(7)]
    pT = psum("pT", [128, 1024], BF16)

    sch = Sched()
    A_KEYS = ["Q", "K", "QK", "qkf", "vc", "kt", "ao", "af", "ap_", "sqs",
              "sq_sb", "S_bf", "Asb", "bv_s", "xb0", "xb1", "xn"]

    def gdma(dst, src, sem, reads=(), writes=(), tag=""):
        return sch.op("gpsimd", lambda e, d=dst, s=src: e.dma_start(d, s),
                      reads=reads, writes=writes, dma_sem=sem, tag=tag)

    def sdma(dst, src, sem, reads=(), writes=(), tag=""):
        return sch.op("sync", lambda e, d=dst, s=src: e.dma_start(d, s),
                      reads=reads, writes=writes, dma_sem=sem, tag=tag)

    _pool = [3, 4, 5, 6]
    _pool_i = [0]
    def pool():
        b = _pool[_pool_i[0] % 4]
        _pool_i[0] += 1
        return b

    class WStream:
        def __init__(self):
            self.items = []; self.emitted = 0
        def push(self, src, tag):
            self.items.append((src, tag))
            return len(self.items) - 1
        def prime(self, n=4):
            while self.emitted < min(n, len(self.items)):
                self._load(self.emitted)
        def _load(self, i):
            src, tag = self.items[i]
            slot = i % 4
            gdma(wsl[slot][:, :], src, f"dw{slot}", writes=[f"wsl{slot}"],
                 tag=f"ld_{tag}")
            self.emitted += 1
        def done_upto(self, i):
            while self.emitted < min(i + 5, len(self.items)):
                self._load(self.emitted)
        def slot(self, i):
            assert i < self.emitted
            return i % 4
    ws = WStream()

    def tile_ap(slab_i, j):
        s = ws.slot(slab_i)
        return wsl[s][:, j * 128:(j + 1) * 128], f"wsl{s}"

    gdma(idn[:, :], identc[:, :], "c0", writes=["idn"])
    gdma(msk[:, :], maskc[:, :], "c1", writes=["msk"])
    gdma(sel[:, :], selc[:, :], "c2", writes=["sel"])
    gdma(c16p[:, :], c16b[:, :], "c3", writes=["c16p"])
    gdma(onn[:, :], onesn[:, :], "c4", writes=["onn"])
    gdma(on1[:, :], ones1[:, :], "c5", writes=["on1"])
    sdma(bq_s[:, :], bqc[:, :], "c6", writes=["bq_s"])
    sdma(bk_s[:, :], bkc[:, :], "c7", writes=["bk_s"])
    sdma(bo_s[:, :], boc[:, :], "c8", writes=["bo_s"])
    sdma(s1_s[:, :], s1c[:, :], "c9", writes=["s1_s"])
    sdma(s2_s[:, :], s2c[:, :], "c10", writes=["s2_s"])
    sdma(hs1_s[:, :], hs1c[:, :], "c11", writes=["hs1_s"])
    sdma(hs2_s[:, :], hs2c[:, :], "c12", writes=["hs2_s"])
    sdma(ws3_s[:, :], ws3c[:, :], "c13", writes=["ws3_s"])
    sdma(flg[:, :], flag[:, :], "c14", writes=["flg"])
    sdma(cst[:, :], cstc[:, :], "c16", writes=["cst"])
    sdma(pj2f[:, :], proj2[:, :], "c15", writes=["pj2f"])
    sch.op("scalar", lambda e: e.activation(pj2s[:, :], pj2f[:, :], AF.Copy,
                                            scale=SCL_PROJ),
           reads=["pj2f"], writes=["pj2s"], tag="pjs")
    sch.op("vector", lambda e: e.memset(S_ext[:, :], 0.0), writes=["S"],
           tag="S0")

    def set_vc_ones():
        v3 = A("vc").rearrange("p (ch e) -> p ch e", e=65)
        sch.op("vector", lambda e, v3=v3: e.memset(v3[:, :, 64:65], 1.0),
               reads=["vc"], writes=["vc"], tag="vc1")

    def load_bv():
        gdma(Asl("bv_s", 0, 1024, 0, 1), bvrc[:, :], "dbv",
             reads=["bv_s"], writes=["bv_s"], tag="bv")

    def tap(name, src_ap, reads):
        if name not in dbg:
            return
        eng = gdma if str(src_ap.dtype) != "dt.float32" else sdma
        eng(dbg[name][:, :], src_ap, f"tp_{name}",
            reads=list(reads), writes=["tapdram"], tag=f"tap_{name}")
    sch.buf("tapdram")

    def ln_group(srcf, skey, xresf, dstf, dkey, mode, tagp, psplit=False,
                 xkey="xres"):
        b1, b2 = pool(), pool()
        for fc in range(NFC):
            sch.op("tensor",
                   lambda e, fc=fc: e.matmul(pb[b1][:, :], onn[:, :], srcf(fc),
                                             start=(fc == 0),
                                             stop=(fc == NFC - 1)),
                   reads=["onn", skey], writes=[f"pb{b1}"], tag=f"{tagp}s1")
        for half in range(2):
            for i in range(4):
                fc = half * 4 + i
                sch.op("vector",
                       lambda e, fc=fc, i=i: e.tensor_mul(
                           Asl("sqs", i * 512, (i + 1) * 512), srcf(fc),
                           srcf(fc)),
                       reads=[skey], writes=["sqs"], tag=f"{tagp}sq")
            for i in range(4):
                fc = half * 4 + i
                sch.op("tensor",
                       lambda e, fc=fc, i=i: e.matmul(
                           pb[b2][:, :], onn[:, :],
                           Asl("sqs", i * 512, (i + 1) * 512),
                           start=(fc == 0), stop=(fc == NFC - 1)),
                       reads=["onn", "sqs"], writes=[f"pb{b2}"],
                       tag=f"{tagp}s2")
        sch.op("vector", lambda e: e.tensor_copy(mB[:, :], pb[b1][:, :]),
               reads=[f"pb{b1}"], writes=["mB"], tag=f"{tagp}mB")
        sch.op("vector", lambda e: e.tensor_mul(vm[:, :], mB[:, :], mB[:, :]),
               reads=["mB"], writes=["vm"], tag=f"{tagp}m2")
        sch.op("vector", lambda e: e.scalar_tensor_tensor(
            vm[:, :], pb[b2][:, :], 1.0, vm[:, :], ALU.mult, ALU.subtract),
               reads=[f"pb{b2}", "vm"], writes=["vm"], tag=f"{tagp}var")
        sch.op("scalar", lambda e: e.activation(vm[:, :], vm[:, :], AF.Ln,
                                                bias=cst[:, 0:1]),
               reads=["vm", "cst"], writes=["vm"], tag=f"{tagp}ln")
        sch.op("scalar", lambda e: e.activation(vm[:, :], vm[:, :], AF.Exp,
                                                scale=-0.5),
               reads=["vm"], writes=["vm"], tag=f"{tagp}ex")
        for fc in range(NFC):
            png = False
            eng = "gpsimd" if png else "vector"
            tsl = (lambda: t32g[:, :]) if png else (lambda: t32[:, :])
            tkey = "t32g" if png else "t32"
            if mode == "xn":
                sch.op(eng,
                       lambda e, fc=fc, tsl=tsl: e.tensor_sub(
                           tsl(), xresf(fc), mB[:, :]),
                       reads=[xkey, "mB"], writes=[tkey], tag=f"{tagp}a1")
                sch.op(eng,
                       lambda e, fc=fc, tsl=tsl: e.tensor_mul(
                           dstf(fc), tsl(), vm[:, :]),
                       reads=[tkey, "vm"], writes=[dkey], tag=f"{tagp}a2")
            else:
                sch.op(eng,
                       lambda e, fc=fc, tsl=tsl: e.tensor_sub(
                           tsl(), srcf(fc), mB[:, :]),
                       reads=[skey, "mB"], writes=[tkey], tag=f"{tagp}a1")
                sch.op(eng,
                       lambda e, fc=fc, tsl=tsl: e.tensor_mul(
                           tsl(), tsl(), vm[:, :]),
                       reads=[tkey, "vm"], writes=[tkey], tag=f"{tagp}a2")
                sch.op(eng,
                       lambda e, fc=fc, tsl=tsl: e.tensor_add(
                           xresf(fc), xresf(fc), tsl()),
                       reads=[tkey, xkey], writes=[xkey],
                       tag=f"{tagp}a3")

    def ln_full(srcf, skey, dstf, dkey, mode, tagp):
        for jg in range(2):
            ln_group(lambda fc, jg=jg: srcf(fc, jg), skey,
                     (lambda fc, jg=jg: xres[:, fc * TT + jg * 512:
                                             fc * TT + jg * 512 + 512]),
                     (lambda fc, jg=jg: dstf(fc, jg)) if dstf else None,
                     dkey, mode, f"{tagp}{jg}")

    def proj_qk(dkey, slab0, bias_sb, bkey, tagp, jgs=(0, 1)):
        for jg in jgs:
            for oc in range(NFC):
                si = slab0 + oc // 4
                bnk = pool()
                for ic in range(NFC):
                    j = (oc % 4) * 8 + ic
                    w_ap, wkey = tile_ap(si, j)
                    sch.op("tensor",
                           lambda e, w_ap=w_ap, ic=ic, jg=jg, bnk=bnk:
                           e.matmul(pb[bnk][:, :], w_ap,
                                    Asl("xn", ic * TT + jg * 512,
                                        ic * TT + jg * 512 + 512),
                                    start=(ic == 0), stop=(ic == NFC - 1)),
                           reads=[wkey, "xn"], writes=[f"pb{bnk}"],
                           tag=f"{tagp}mm")
                qk = 0 if dkey == "Q" else 1024
                sch.op("scalar",
                       lambda e, oc=oc, jg=jg, bnk=bnk, qk=qk: e.activation(
                           Asl("QK", oc * 2048 + qk + jg * 512,
                               oc * 2048 + qk + jg * 512 + 512),
                           pb[bnk][:, :], AF.Identity,
                           bias=bias_sb[:, oc:oc + 1]),
                       reads=[f"pb{bnk}", bkey], writes=[dkey],
                       tag=f"{tagp}ev")

    def v_upfront(slab0, tagp):
        for c in range(8):
            for og in range(2):
                si = slab0 + og
                bnk = pool()
                s = ws.slot(si)
                rhs = wsl[s].rearrange("p (o4 ic o) -> p o4 ic o", o4=4, ic=8)
                for ic in range(NFC):
                    sch.op("tensor",
                           lambda e, c=c, ic=ic, rhs=rhs, bnk=bnk: e.matmul(
                               pb[bnk][:, :],
                               Asl("xn", ic * TT + c * 128,
                                   ic * TT + (c + 1) * 128),
                               rhs[:, 0:4, ic, :],
                               start=(ic == 0), stop=False),
                           reads=[f"wsl{s}", "xn"], writes=[f"pb{bnk}"],
                           tag=f"{tagp}mm")
                sch.op("tensor",
                       lambda e, og=og, bnk=bnk: e.matmul(
                           pb[bnk][:, :], on1[0:1, :],
                           Asl("bv_s", og * 512, (og + 1) * 512, 0, 1),
                           start=False, stop=True),
                       reads=["on1", "bv_s"], writes=[f"pb{bnk}"],
                       tag=f"{tagp}b")
                v3 = A("vc").rearrange("p (ch e) -> p ch e", e=65)
                sch.op("scalar",
                       lambda e, c=c, og=og, bnk=bnk, v3=v3: e.activation(
                           v3[:, c * H + og * 8:c * H + (og + 1) * 8, 0:64],
                           pb[bnk][:, :], AF.Copy),
                       reads=[f"pb{bnk}"], writes=["vc"], tag=f"{tagp}ev")

    def phi_group(t0, with_q, tagp):
        def qk_ap(fc, r0, r1, qk, c0, c1):
            return Asl("QK", fc * 2048 + qk * 1024 + t0 + c0,
                       fc * 2048 + qk * 1024 + t0 + c1, r0, r1)

        def emit_sq(hp):
            so = (hp % 4) * 512
            if with_q:
                sch.op("vector",
                       lambda e, hp=hp, so=so: e.tensor_mul(
                           Asl("sqs", so, so + 256),
                           qk_ap(hp, 0, 128, 0, 0, 256),
                           qk_ap(hp, 0, 128, 0, 0, 256)),
                       reads=["Q"], writes=["sqs"], tag=f"{tagp}q2")
            sch.op("vector",
                   lambda e, hp=hp, so=so: e.tensor_mul(
                       Asl("sqs", so + 256, so + 512),
                       qk_ap(hp, 0, 128, 1, 0, 256),
                       qk_ap(hp, 0, 128, 1, 0, 256)),
                   reads=["K"], writes=["sqs"], tag=f"{tagp}k2")

        def emit_sqmm(hp):
            so = (hp % 4) * 512
            bnk = pool()
            sqc0 = 0 if with_q else 256
            sch.op("tensor",
                   lambda e, bnk=bnk, sqc0=sqc0, so=so: e.matmul(
                       pb[bnk][0:2, sqc0:512], c16p[:, :],
                       Asl("sqs", so + sqc0, so + 512),
                       start=True, stop=True),
                   reads=["c16p", "sqs"], writes=[f"pb{bnk}"],
                   tag=f"{tagp}sqm")
            sch.op("vector",
                   lambda e, bnk=bnk, sqc0=sqc0, so=so: e.tensor_copy(
                       Asl("sq_sb", so + sqc0, so + 512, 0, 2),
                       pb[bnk][0:2, sqc0:512]),
                   reads=[f"pb{bnk}"], writes=["sq_sb"], tag=f"{tagp}sqe")

        def emit_u(hp):
            bo = (hp % 4) * 512
            for hh in range(2):
                h = 2 * hp + hh
                base = 64 * hh
                selsl = sel[:, hh * 128:(hh + 1) * 128]
                if with_q:
                    rhs = A("QK").rearrange("p (fc a t) -> p fc a t",
                                            a=2, t=TT)[
                        base:base + 64, hp, :, t0:t0 + 256]
                    bnk = pool()
                    sch.op("tensor",
                           lambda e, rhs=rhs, base=base, bnk=bnk: e.matmul(
                               pb[bnk][:, 0:512],
                               pj2s[base:base + 64, :], rhs,
                               start=True, stop=False,
                               tile_position=(base, 0)),
                           reads=["pj2s", "Q", "K"], writes=[f"pb{bnk}"],
                           tag=f"{tagp}qku")
                    sch.op("tensor",
                           lambda e, selsl=selsl, bnk=bnk, bo=bo: e.matmul(
                               pb[bnk][:, 0:512], selsl,
                               Asl("sq_sb", bo, bo + 512, 0, 2),
                               start=False, stop=True),
                           reads=["sel", "sq_sb"], writes=[f"pb{bnk}"],
                           tag=f"{tagp}qkn")
                    sch.op("scalar",
                           lambda e, h=h, bnk=bnk: e.activation(
                               Asl("qkf", h * 512, (h + 1) * 512),
                               pb[bnk][:, 0:512], AF.Exp, bias=cst[:, 1:2]),
                           reads=[f"pb{bnk}", "cst"], writes=["qkf"],
                           tag=f"{tagp}qke")
                bnk = pool()
                for cc in range(2):
                    sch.op("tensor",
                           lambda e, hp=hp, base=base, cc=cc, bnk=bnk:
                           e.matmul(pb[bnk][:, cc * 128:(cc + 1) * 128],
                                    qk_ap(hp, base, base + 64, 1, cc * 128,
                                          (cc + 1) * 128),
                                    pj2s[base:base + 64, :],
                                    start=True, stop=False,
                                    tile_position=(base, 0)),
                           reads=["K", "pj2s"], writes=[f"pb{bnk}"],
                           tag=f"{tagp}tu")
                    sch.op("tensor",
                           lambda e, cc=cc, selsl=selsl, bnk=bnk, bo=bo:
                           e.matmul(pb[bnk][:, cc * 128:(cc + 1) * 128],
                                    Asl("sq_sb", bo + 256 + cc * 128,
                                        bo + 256 + (cc + 1) * 128, 0, 2),
                                    selsl, start=False, stop=True),
                           reads=["sq_sb", "sel"], writes=[f"pb{bnk}"],
                           tag=f"{tagp}tn")
                sch.op("scalar",
                       lambda e, h=h, bnk=bnk: e.activation(
                           Asl("kt", h * 256, (h + 1) * 256),
                           pb[bnk][:, 0:256], AF.Exp, bias=cst[:, 1:2]),
                       reads=[f"pb{bnk}", "cst"], writes=["kt"],
                       tag=f"{tagp}te")

        for step in range(NFC + 2):
            if step < NFC:
                emit_sq(step)
            if step >= 1 and step - 1 < NFC:
                emit_sqmm(step - 1)
            if step >= 2:
                emit_u(step - 2)
        if with_q:
            sch.op("vector",
                   lambda e: e.tensor_scalar_add(A("qkf"), A("qkf"), PEPS),
                   reads=["qkf"], writes=["qkf"], tag=f"{tagp}ep")
        sch.op("vector",
               lambda e: e.tensor_scalar_add(A("kt"), A("kt"), PEPS),
               reads=["kt"], writes=["kt"], tag=f"{tagp}ept")

    def s_bank(h):
        if h < 7: return 0, h * 65
        if h < 14: return 1, (h - 7) * 65
        return 2, (h - 14) * 65

    sbank_started = {}
    def s_start(bk):
        if sbank_started.get(bk):
            return False
        sbank_started[bk] = True
        return True

    def s_reset():
        sbank_started.clear()

    def s_carry_in(tagp):
        sch.op("vector", lambda e: e.tensor_copy(A("S_bf"), S_ext[:, :]),
               reads=["S"], writes=["S_bf"], tag=f"{tagp}cbf")
        s_reset()
        for h in range(H):
            bk, c0 = s_bank(h)
            st = s_start(bk)
            sch.op("tensor",
                   lambda e, h=h, bk=bk, c0=c0, st=st: e.matmul(
                       pb[bk][:, c0:c0 + 65], idn[:, :],
                       Asl("S_bf", h * 65, (h + 1) * 65),
                       start=st, stop=False),
                   reads=["idn", "S_bf"], writes=[f"pb{bk}"], tag=f"{tagp}ci")

    def s_carry_out(tagp):
        sch.op("vector", lambda e: e.tensor_copy(S_ext[:, 0:455],
                                                 pb[0][:, 0:455]),
               reads=["pb0"], writes=["S"], tag=f"{tagp}co0")
        sch.op("vector", lambda e: e.tensor_copy(S_ext[:, 455:910],
                                                 pb[1][:, 0:455]),
               reads=["pb1"], writes=["S"], tag=f"{tagp}co1")
        sch.op("vector", lambda e: e.tensor_copy(S_ext[:, 910:1040],
                                                 pb[2][:, 0:130]),
               reads=["pb2"], writes=["S"], tag=f"{tagp}co2")

    def s_cast(tagp):
        sch.op("scalar", lambda e: e.activation(Asl("S_bf", 0, 455),
                                                pb[0][:, 0:455], AF.Copy),
               reads=["pb0"], writes=["S_bf"], tag=f"{tagp}sc0")
        sch.op("scalar", lambda e: e.activation(Asl("S_bf", 455, 910),
                                                pb[1][:, 0:455], AF.Copy),
               reads=["pb1"], writes=["S_bf"], tag=f"{tagp}sc1")
        sch.op("scalar", lambda e: e.activation(Asl("S_bf", 910, 1040),
                                                pb[2][:, 0:130], AF.Copy),
               reads=["pb2"], writes=["S_bf"], tag=f"{tagp}sc2")

    def attn_chunk(c, cc, full, tagp):
        def emit_T1(h):
            qsl = Asl("qkf", h * 512 + cc * 128, h * 512 + (cc + 1) * 128)
            ksl = Asl("qkf", h * 512 + 256 + cc * 128,
                      h * 512 + 256 + (cc + 1) * 128)
            bA = 3 + (h % 2)
            sch.op("tensor",
                   lambda e, ksl=ksl, qsl=qsl, bA=bA: e.matmul(
                       pb[bA][:, 0:128], ksl, qsl, start=True, stop=True),
                   reads=["qkf"], writes=[f"pb{bA}"], tag=f"{tagp}T1")

        def emit_dv(h):
            aosl = Asl("ao", (c % 4) * 1024 + h * 64,
                       (c % 4) * 1024 + (h + 1) * 64)
            rk = "recL" if h < 8 else "recH"
            aok = f"ao{c % 4}_{h // 2}"
            sch.op("scalar",
                   lambda e, aosl=aosl, h=h: e.activation(
                       aosl, aosl, AF.Copy, scale=rec[:, h:h + 1]),
                   reads=[aok, rk], writes=[aok], tag=f"{tagp}dv")

        def emit_tr(fc):
            sch.op("sync",
                   lambda e, fc=fc: e.dma_start_transpose(
                       Asl("af", fc * 512 + (c % 4) * 128,
                           fc * 512 + (c % 4) * 128 + 128),
                       Asl("ao", (c % 4) * 1024 + fc * 128,
                           (c % 4) * 1024 + (fc + 1) * 128)),
                   reads=[f"ao{c % 4}_{fc}"], writes=["af"], dma_sem="dtr",
                   tag=f"{tagp}tr")

        if full:
            emit_T1(0)
        for h in range(H):
            qsl = Asl("qkf", h * 512 + cc * 128, h * 512 + (cc + 1) * 128)
            ktsl = Asl("kt", h * 256 + cc * 128, h * 256 + (cc + 1) * 128)
            vsl = Asl("vc", (c * H + h) * 65, (c * H + h + 1) * 65)
            bk, c0 = s_bank(h)
            if full:
                if h + 1 < H:
                    emit_T1(h + 1)
                bA = 3 + (h % 2)
                asb = Asl("Asb", (h % 2) * 128, (h % 2) * 128 + 128)
                sch.op("vector",
                       lambda e, asb=asb, bA=bA: e.tensor_mul(
                           asb, pb[bA][:, 0:128], msk[:, :]),
                       reads=[f"pb{bA}", "msk"], writes=["Asb"],
                       tag=f"{tagp}mk")
            st = s_start(bk)
            sch.op("tensor",
                   lambda e, ktsl=ktsl, vsl=vsl, bk=bk, c0=c0, st=st:
                   e.matmul(pb[bk][:, c0:c0 + 65], ktsl, vsl,
                            start=st, stop=False),
                   reads=["kt", "vc"], writes=[f"pb{bk}"], tag=f"{tagp}T4")
            if full:
                bN = 5 + (h % 2)
                sch.op("tensor",
                       lambda e, asb=asb, vsl=vsl, bN=bN: e.matmul(
                           pb[bN][:, 0:65], asb, vsl, start=True, stop=False),
                       reads=["Asb", "vc"], writes=[f"pb{bN}"],
                       tag=f"{tagp}T2")
                sch.op("tensor",
                       lambda e, qsl=qsl, h=h, bN=bN: e.matmul(
                           pb[bN][:, 0:65], qsl,
                           Asl("S_bf", h * 65, (h + 1) * 65),
                           start=False, stop=True),
                       reads=["qkf", "S_bf"], writes=[f"pb{bN}"],
                       tag=f"{tagp}T3")
                dcol = den[:, h:h + 1]
                sch.op("vector",
                       lambda e, dcol=dcol, bN=bN: e.tensor_scalar_add(
                           dcol, pb[bN][:, 64:65], AEPS),
                       reads=[f"pb{bN}"], writes=["den"], tag=f"{tagp}d1")
                aosl = Asl("ao", (c % 4) * 1024 + h * 64,
                           (c % 4) * 1024 + (h + 1) * 64)
                aok = f"ao{c % 4}_{h // 2}"
                sch.op("vector",
                       lambda e, aosl=aosl, bN=bN: e.tensor_copy(
                           aosl, pb[bN][:, 0:64]),
                       reads=[f"pb{bN}"], writes=[aok], tag=f"{tagp}nc")
                if h == 7:
                    sch.op("vector",
                           lambda e: e.reciprocal(rec[:, 0:8], den[:, 0:8]),
                           reads=["den"], writes=["recL"], tag=f"{tagp}rcL")
                if h >= 8:
                    emit_dv(h - 8)
                    if h % 2 == 1:
                        emit_tr((h - 9) // 2)
        if full:
            sch.op("vector",
                   lambda e: e.reciprocal(rec[:, 8:16], den[:, 8:16]),
                   reads=["den"], writes=["recH"], tag=f"{tagp}rcH")
            s_cast(tagp)
            for j in range(4):
                emit_dv(8 + 2 * j)
                emit_dv(9 + 2 * j)
                emit_tr(4 + j)

    def wo_proj(g, slab0, tagp):
        for oc in range(NFC):
            si = slab0 + oc // 4
            bnk = pool()
            for ic in range(NFC):
                j = (oc % 4) * 8 + ic
                w_ap, wkey = tile_ap(si, j)
                sch.op("tensor",
                       lambda e, w_ap=w_ap, ic=ic, bnk=bnk: e.matmul(
                           pb[bnk][:, :], w_ap,
                           Asl("af", ic * 512, (ic + 1) * 512),
                           start=(ic == 0), stop=(ic == NFC - 1)),
                       reads=[wkey, "af"], writes=[f"pb{bnk}"],
                       tag=f"{tagp}mm")
            sch.op("scalar",
                   lambda e, oc=oc, bnk=bnk: e.activation(
                       Asl("ap_", oc * 512, (oc + 1) * 512), pb[bnk][:, :],
                       AF.Identity, bias=bo_s[:, oc:oc + 1]),
                   reads=[f"pb{bnk}", "bo_s"], writes=["ap_"],
                   tag=f"{tagp}ev")

    def ln3(g, tagp):
        ln_group(lambda fc: Asl("ap_", fc * 512, (fc + 1) * 512), "ap_",
                 lambda fc: xres[:, fc * TT + g * 512:
                                 fc * TT + (g + 1) * 512],
                 None, None, "res", f"{tagp}ln", xkey=f"xres{g}")

    def xsrc(src, t):
        return src.rearrange("(fc p) t -> p fc t", p=128)[
            :, :, t * TT:(t + 1) * TT]

    pfx = [ws.push(wdr["wk"][i], f"pwk{i}") for i in range(2)] + \
          [ws.push(wdr["wv"][i], f"pwv{i}") for i in range(2)]
    tile_slabs = []
    for t in range(NT):
        d = {}
        d["wq"] = [ws.push(wdr["wq"][i], f"t{t}wq{i}") for i in range(2)]
        d["wk"] = [ws.push(wdr["wk"][i], f"t{t}wk{i}") for i in range(2)]
        d["wv"] = [ws.push(wdr["wv"][i], f"t{t}wv{i}") for i in range(2)]
        d["wo"] = [ws.push(wdr["wo"][i], f"t{t}wo{i}") for i in range(2)]
        s1l = []
        for b in range(8):
            s1l.append(ws.push(wdr["w1"][b], f"t{t}w1_{b}"))
            s1l.append(ws.push(wdr["w2"][b], f"t{t}w2_{b}"))
        d["s1"] = s1l
        d["h1"] = [ws.push(wdr["h1"][i], f"t{t}h1_{i}") for i in range(32)]
        d["h2"] = [ws.push(wdr["h2"][i], f"t{t}h2_{i}") for i in range(32)]
        d["w3"] = [ws.push(wdr["w3"][i], f"t{t}w3_{i}") for i in range(8)]
        tile_slabs.append(d)
    ws.prime(4)

    set_vc_ones()
    load_bv()

    s_reset()
    for pt in range(2):
        tg = f"p{pt}"
        for jg in range(2):
            sdma(xres.rearrange("p (fc t) -> p fc t", t=TT)[
                     :, :, jg * 512:(jg + 1) * 512],
                 xsrc(xp, pt)[:, :, jg * 512:(jg + 1) * 512], f"dx{jg}",
                 reads=[f"xres{jg}"], writes=[f"xres{jg}"],
                 tag=f"{tg}x{jg}")
            gdma(A("xb").rearrange("p (fc t) -> p fc t", t=TT)[
                     :, :, jg * 512:(jg + 1) * 512],
                 xsrc(xp, pt)[:, :, jg * 512:(jg + 1) * 512], f"dxb{jg}",
                 reads=[f"xb{jg}"], writes=[f"xb{jg}"], tag=f"{tg}xb{jg}")
        for jg in range(2):
            ln_group(lambda fc, jg=jg: Asl("xb", fc * TT + jg * 512,
                                           fc * TT + jg * 512 + 512),
                     f"xb{jg}",
                     lambda fc, jg=jg: xres[:, fc * TT + jg * 512:
                                            fc * TT + jg * 512 + 512],
                     lambda fc, jg=jg: Asl("xn", fc * TT + jg * 512,
                                           fc * TT + jg * 512 + 512),
                     "xn", "xn", f"{tg}l1_{jg}", psplit=True,
                     xkey=f"xres{jg}")
            proj_qk("K", pfx[0], bk_s, "bk_s", f"{tg}K", jgs=(jg,))
        if pt == 0:
            tap("pxn0", A("xn"), ["xn"])
        if pt == 1:
            ws.done_upto(pfx[1])
        v_upfront(pfx[2], f"{tg}V")
        if pt == 1:
            ws.done_upto(pfx[3])
        if pt == 0:
            tap("pvc0", A("vc"), ["vc"])
        for g in range(4):
            phi_group(g * 256, False, f"{tg}f{g}")
            if pt == 0 and g == 0:
                tap("pkt0", A("kt"), ["kt"])
            for cc in range(2):
                attn_chunk(g * 2 + cc, cc, False, f"{tg}c{g * 2 + cc}")
    s_carry_out("pco")
    sch.op("vector", lambda e: e.tensor_scalar_mul(S_ext[:, :], S_ext[:, :],
                                                   flg[:, :]),
           reads=["S", "flg"], writes=["S"], tag="Sflag")
    tap("Spfx", S_ext[:, :], ["S"])

    for t in range(NT):
        tg = f"t{t}"
        sl = tile_slabs[t]
        sch.claim_many(A_KEYS, ["hb", "g1", "yb"])
        if t > 0:
            set_vc_ones()
            load_bv()
        for jg in range(2):
            sdma(xres.rearrange("p (fc t) -> p fc t", t=TT)[
                     :, :, jg * 512:(jg + 1) * 512],
                 xsrc(xo, t)[:, :, jg * 512:(jg + 1) * 512], f"dx{jg}",
                 reads=[f"xres{jg}"], writes=[f"xres{jg}"],
                 tag=f"{tg}x{jg}")
            gdma(A("xb").rearrange("p (fc t) -> p fc t", t=TT)[
                     :, :, jg * 512:(jg + 1) * 512],
                 xsrc(xo, t)[:, :, jg * 512:(jg + 1) * 512], f"dxb{jg}",
                 reads=[f"xb{jg}"], writes=[f"xb{jg}"], tag=f"{tg}xb{jg}")
        for jg in range(2):
            ln_group(lambda fc, jg=jg: Asl("xb", fc * TT + jg * 512,
                                           fc * TT + jg * 512 + 512),
                     f"xb{jg}",
                     lambda fc, jg=jg: xres[:, fc * TT + jg * 512:
                                            fc * TT + jg * 512 + 512],
                     lambda fc, jg=jg: Asl("xn", fc * TT + jg * 512,
                                           fc * TT + jg * 512 + 512),
                     "xn", "xn", f"{tg}l1_{jg}", psplit=True,
                     xkey=f"xres{jg}")
            proj_qk("Q", sl["wq"][0], bq_s, "bq_s", f"{tg}Q", jgs=(jg,))
        if t == 0:
            tap("xn0", A("xn"), ["xn"])
        ws.done_upto(sl["wq"][1])
        proj_qk("K", sl["wk"][0], bk_s, "bk_s", f"{tg}K")
        ws.done_upto(sl["wk"][1])

        v_upfront(sl["wv"][0], f"{tg}V")
        ws.done_upto(sl["wv"][1])
        if t == 0:
            tap("vc0", A("vc"), ["vc"])
        s_carry_in(f"{tg}ci")
        pend_ln3 = None
        for g in range(4):
            phi_group(g * 256, True, f"{tg}f{g}")
            if pend_ln3 is not None:
                pend_ln3(); pend_ln3 = None
            if t == 0 and g == 0:
                tap("kt0", A("kt"), ["kt"])
            for cc in range(2):
                c = g * 2 + cc
                attn_chunk(c, cc, True, f"{tg}c{c}")
            if g % 2 == 1:
                if t == 0 and g == 1:
                    tap("ao0", A("ao"), ["ao"]); tap("af0", A("af"), ["af"])
                wo_proj(g // 2, sl["wo"][0], f"{tg}wo{g // 2}")
                pend_ln3 = (lambda gg=g // 2: ln3(gg, f"{tg}wo{gg}"))
                if t == 0 and g == 1:
                    tap("ap0", A("ap_"), ["ap_"])
        if pend_ln3 is not None:
            pend_ln3(); pend_ln3 = None
        ws.done_upto(sl["wo"][1])
        s_carry_out(f"{tg}co")
        if t == 0:
            tap("x2_0", xres[:, :], ["xres0", "xres1"])
            tap("S_t0", S_ext[:, :], ["S"])

        sch.claim_many(["hb", "g1"],
                       ["Q", "K", "QK", "qkf", "vc", "kt", "ao", "af", "ap_",
                        "S_bf", "Asb", "bv_s"])
        for jg in range(2):
            for fc in range(NFC):
                sch.op("vector",
                       lambda e, fc=fc, jg=jg: e.tensor_copy(
                           Asl("xb", fc * TT + jg * 512,
                               fc * TT + jg * 512 + 512),
                           xres[:, fc * TT + jg * 512:
                                fc * TT + jg * 512 + 512]),
                       reads=[f"xres{jg}"], writes=[f"xb{jg}"],
                       tag=f"{tg}l2c")
            ln_group(lambda fc, jg=jg: Asl("xb", fc * TT + jg * 512,
                                           fc * TT + jg * 512 + 512),
                     f"xb{jg}",
                     lambda fc, jg=jg: xres[:, fc * TT + jg * 512:
                                            fc * TT + jg * 512 + 512],
                     lambda fc, jg=jg: Asl("xn", fc * TT + jg * 512,
                                           fc * TT + jg * 512 + 512),
                     "xn", "xn", f"{tg}l2_{jg}", psplit=True,
                     xkey=f"xres{jg}")
        if t == 0:
            tap("xn2_0", A("xn"), ["xn"])
        sch.claim_many(["hb"], ["xb0", "xb1", "sqs", "sq_sb"])
        for b in range(8):
            i1, i2 = sl["s1"][2 * b], sl["s1"][2 * b + 1]
            for o4 in range(4):
                oc = b * 4 + o4
                for jg in range(2):
                    b1, b2 = pool(), pool()
                    for ic in range(NFC):
                        j = o4 * 8 + ic
                        w1ap, w1k = tile_ap(i1, j)
                        w2ap, w2k = tile_ap(i2, j)
                        sch.op("tensor",
                               lambda e, w1ap=w1ap, ic=ic, jg=jg, b1=b1:
                               e.matmul(pb[b1][:, :], w1ap,
                                        Asl("xn", ic * TT + jg * 512,
                                            ic * TT + jg * 512 + 512),
                                        start=(ic == 0),
                                        stop=(ic == NFC - 1)),
                               reads=[w1k, "xn"], writes=[f"pb{b1}"],
                               tag=f"{tg}s1a")
                        sch.op("tensor",
                               lambda e, w2ap=w2ap, ic=ic, jg=jg, b2=b2:
                               e.matmul(pb[b2][:, :], w2ap,
                                        Asl("xn", ic * TT + jg * 512,
                                            ic * TT + jg * 512 + 512),
                                        start=(ic == 0),
                                        stop=(ic == NFC - 1)),
                               reads=[w2k, "xn"], writes=[f"pb{b2}"],
                               tag=f"{tg}s1b")
                    sch.op("scalar",
                           lambda e, oc=oc, b1=b1: e.activation(
                               t32[:, :], pb[b1][:, :], AF.Silu,
                               scale=s1_s[:, oc:oc + 1]),
                           reads=[f"pb{b1}", "s1_s"], writes=["t32"],
                           tag=f"{tg}si")
                    sch.op("vector",
                           lambda e, oc=oc, jg=jg, b2=b2:
                           e.scalar_tensor_tensor(
                               Asl("hb", oc * TT + jg * 512,
                                   oc * TT + jg * 512 + 512),
                               pb[b2][:, :], s2_s[:, oc:oc + 1], t32[:, :],
                               ALU.mult, ALU.mult),
                           reads=[f"pb{b2}", "s2_s", "t32"], writes=["hb"],
                           tag=f"{tg}hm")
            ws.done_upto(i2)
        if t == 0:
            tap("h0", A("hb"), ["hb"])
        sch.claim_many(["g1"], ["xn"])

        def hstage(slabs, skey, src_name, dkey, dst_name, scol, sckey, tagp):
            for oc in range(NHC):
                si = slabs[oc]
                for jg in range(2):
                    bnk = pool()
                    for ic in range(NHC):
                        w_ap, wkey = tile_ap(si, ic)
                        sch.op("tensor",
                               lambda e, w_ap=w_ap, ic=ic, jg=jg, bnk=bnk:
                               e.matmul(pb[bnk][:, :], w_ap,
                                        Asl(src_name, ic * TT + jg * 512,
                                            ic * TT + jg * 512 + 512),
                                        start=(ic == 0),
                                        stop=(ic == NHC - 1)),
                               reads=[wkey, skey], writes=[f"pb{bnk}"],
                               tag=f"{tagp}mm")
                    sch.op("scalar",
                           lambda e, oc=oc, jg=jg, bnk=bnk: e.activation(
                               Asl(dst_name, oc * TT + jg * 512,
                                   oc * TT + jg * 512 + 512),
                               pb[bnk][:, :], AF.Gelu,
                               scale=scol[:, oc:oc + 1]),
                           reads=[f"pb{bnk}", sckey], writes=[dkey],
                           tag=f"{tagp}ev")
                ws.done_upto(si)
        hstage(sl["h1"], "hb", "hb", "g1", "g1", hs1_s, "hs1_s", f"{tg}H1")
        if t == 0:
            tap("g10", A("g1"), ["g1"])
        hstage(sl["h2"], "g1", "g1", "hb", "hb", hs2_s, "hs2_s", f"{tg}H2")
        if t == 0:
            tap("g20", A("hb"), ["hb"])
        sch.claim_many(["yb"], ["g1"])
        for oc in range(NFC):
            si = sl["w3"][oc]
            for jg in range(2):
                bnk = pool()
                for ic in range(NHC):
                    w_ap, wkey = tile_ap(si, ic)
                    sch.op("tensor",
                           lambda e, w_ap=w_ap, ic=ic, jg=jg, bnk=bnk:
                           e.matmul(pb[bnk][:, :], w_ap,
                                    Asl("hb", ic * TT + jg * 512,
                                        ic * TT + jg * 512 + 512),
                                    start=(ic == 0), stop=(ic == NHC - 1)),
                           reads=[wkey, "hb"], writes=[f"pb{bnk}"],
                           tag=f"{tg}w3mm")
                sch.op("vector",
                       lambda e, oc=oc, jg=jg, bnk=bnk: e.tensor_scalar_mul(
                           Asl("yb", oc * TT + jg * 512,
                               oc * TT + jg * 512 + 512),
                           pb[bnk][:, :], ws3_s[:, oc:oc + 1]),
                       reads=[f"pb{bnk}", "ws3_s"], writes=["yb"],
                       tag=f"{tg}w3ev")
            ws.done_upto(si)
        if t == 0:
            tap("y0", A("yb"), ["yb"])
        sch.claim_many(["sqs", "hb"], ["hb"])
        for jg in range(2):
            ln_group(lambda fc, jg=jg: Asl("yb", fc * TT + jg * 512,
                                           fc * TT + jg * 512 + 512), "yb",
                     lambda fc, jg=jg: xres[:, fc * TT + jg * 512:
                                            fc * TT + jg * 512 + 512],
                     None, None, "res", f"{tg}l4_{jg}", psplit=True,
                     xkey=f"xres{jg}")
        sdma(out.rearrange("(fc p) t -> p fc t", p=128)[
                 :, :, t * TT:(t + 1) * TT],
             xres.rearrange("p (fc t) -> p fc t", t=TT), "dout",
             reads=["xres0", "xres1"], writes=["outdram"], tag=f"{tg}out")

    sch.op("sync", lambda e: None, reads=["outdram", "tapdram"], tag="fin")
    sch.emit(nc, es)
    es.close()
    return nc


_prog_cache = {}

def _get_program():
    if "p" not in _prog_cache:
        _prog_cache["p"] = build_program(debug=_DEBUG)
    return _prog_cache["p"]


def kernel(L0_attn_x, L0_attn_Wq, L0_attn_bq, L0_attn_Wk, L0_attn_bk,
           L0_attn_Wv, L0_attn_bv, L0_attn_Wo, L0_attn_bo, L0_attn_proj,
           L0_ffn_w1_w, L0_ffn_w1_s, L0_ffn_w2_w, L0_ffn_w2_s,
           L0_ffn_h1_w, L0_ffn_h1_s, L0_ffn_h2_w, L0_ffn_h2_s,
           L0_ffn_w3_w, L0_ffn_w3_s):
    f32 = lambda a: np.ascontiguousarray(np.asarray(a, dtype=np.float32))
    x = f32(L0_attn_x)
    pj = f32(L0_attn_proj)
    selv = np.zeros((2, 256), np.float32)
    selv[0, 0:128] = -1.0; selv[1, 128:256] = -1.0
    c16bv = np.zeros((128, 2), np.float32)
    c16bv[0:64, 0] = C16; c16bv[64:128, 1] = C16
    shared = {
        "wq": _blocked(f32(L0_attn_Wq)), "wk": _blocked(f32(L0_attn_Wk)),
        "wv": _blocked(f32(L0_attn_Wv)), "wo": _blocked(f32(L0_attn_Wo)),
        "w1": _blocked(f32(L0_ffn_w1_w)), "w2": _blocked(f32(L0_ffn_w2_w)),
        "h1": _blocked(f32(L0_ffn_h1_w)), "h2": _blocked(f32(L0_ffn_h2_w)),
        "w3": _blocked(f32(L0_ffn_w3_w)),
        "proj2": np.ascontiguousarray(np.concatenate([pj.T, pj.T], axis=0)),
        "identc": np.eye(128, dtype=np.float32),
        "maskc": np.ascontiguousarray(np.triu(np.ones((128, 128),
                                                      np.float32))),
        "selc": selv, "c16b": c16bv,
        "onesn": np.full((128, 128), INV_D, np.float32),
        "ones1": np.ones((1, 128), np.float32),
        "bqc": _pcol(L0_attn_bq), "bkc": _pcol(L0_attn_bk),
        "boc": _pcol(L0_attn_bo),
        "bvrc": f32(L0_attn_bv).reshape(1, 1024),
        "s1c": _pcol(L0_ffn_w1_s), "s2c": _pcol(L0_ffn_w2_s),
        "hs1c": _pcol(L0_ffn_h1_s), "hs2c": _pcol(L0_ffn_h2_s),
        "ws3c": _pcol(L0_ffn_w3_s),
        "cstc": np.ascontiguousarray(np.broadcast_to(
            np.array([LNEPS, EXP_BIAS], np.float32), (128, 2))),
    }
    in_maps = []
    for c in range(8):
        b, half = c // 2, c % 2
        m = dict(shared)
        m["xo"] = np.ascontiguousarray(x[b, half * TPC:(half + 1) * TPC, :].T)
        if half == 0:
            m["xp"] = np.zeros((D, TPC), np.float32)
            m["flag"] = np.zeros((128, 1), np.float32)
        else:
            m["xp"] = np.ascontiguousarray(x[b, 0:TPC, :].T)
            m["flag"] = np.ones((128, 1), np.float32)
        in_maps.append(m)
    nc = _get_program()
    res = run_bass_kernel_spmd(nc, in_maps, core_ids=list(range(8)))
    globals()["_last_results"] = res.results
    outp = np.empty((4, 4096, 1024), np.float32)
    for c in range(8):
        b, half = c // 2, c % 2
        outp[b, half * TPC:(half + 1) * TPC, :] = res.results[c]["out"].T
    return outp


# version 12
# speedup vs baseline: 1.0785x; 1.0328x over previous
import numpy as np
import ml_dtypes
from contextlib import ExitStack

import concourse.bass as bass
import concourse.mybir as mybir
from concourse.bass_utils import run_bass_kernel_spmd

AF = mybir.ActivationFunctionType
ALU = mybir.AluOpType
F32 = mybir.dt.float32
BF16 = mybir.dt.bfloat16

D = 1024; H = 16; DK = 64; M = 128; DH = 4096
TPC = 2048; TT = 1024; NT = 2
NFC = 8; NHC = 32
LNEPS = 1e-5; PEPS = 1e-6; AEPS = 1e-6
EXP_BIAS = -0.5 * float(np.log(M))
INV_D = 1.0 / D
C16 = 0.5 / float(np.sqrt(DK))
SCL_PROJ = float(DK) ** -0.25

_DEBUG = ()

TAP_SHAPES = {
    "xn0": (128, 8192), "q0": (128, 8192), "k0": (128, 8192),
    "vc0": (128, 8320), "qf0": (128, 4096), "kf0": (128, 4096),
    "kt0": (128, 4096), "Spfx": (128, 1040), "ao0": (128, 4096),
    "ap0": (128, 4096), "x2_0": (128, 8192), "S_t0": (128, 1040),
    "h0": (128, 32768), "g10": (128, 32768), "g20": (128, 32768),
    "y0": (128, 8192), "af0": (128, 4096), "xn2_0": (128, 8192),
    "pkt0": (128, 4096), "pvc0": (128, 8320), "pxn0": (128, 8192),
}

ENGINES = ("tensor", "scalar", "vector", "gpsimd", "sync")


class _Buf:
    __slots__ = ("writer", "readers")
    def __init__(self):
        self.writer = None; self.readers = {}


class _Op:
    __slots__ = ("eng", "fn", "idx", "deps", "dma_sem", "need_inc", "inc_val",
                 "tag")
    def __init__(self, eng, fn, idx, dma_sem, tag):
        self.eng = eng; self.fn = fn; self.idx = idx
        self.deps = []; self.dma_sem = dma_sem
        self.need_inc = False; self.inc_val = None; self.tag = tag


class Sched:
    def __init__(self):
        self.ops = []; self.bufs = {}; self.dma_count = {}

    def buf(self, name):
        if name not in self.bufs:
            self.bufs[name] = _Buf()
        return self.bufs[name]

    def op(self, eng, fn, reads=(), writes=(), dma_sem=None, tag=""):
        o = _Op(eng, fn, len(self.ops), dma_sem, tag)
        self.ops.append(o)
        ekey = ("dma", dma_sem, eng) if dma_sem is not None else eng
        if dma_sem is not None:
            self.dma_count[dma_sem] = self.dma_count.get(dma_sem, 0) + 1
        deps = set()
        for r in reads:
            b = self.buf(r)
            if b.writer is not None:
                w = self.ops[b.writer]
                if w.idx != o.idx and (w.eng != eng or w.dma_sem is not None
                                       or dma_sem is not None):
                    deps.add(b.writer)
            if b.readers.get(ekey, -1) < o.idx:
                b.readers[ekey] = o.idx
        for wn in writes:
            b = self.buf(wn)
            if b.writer is not None:
                w = self.ops[b.writer]
                if w.idx != o.idx and (w.eng != eng or w.dma_sem is not None
                                       or dma_sem is not None):
                    deps.add(b.writer)
            for rk, ridx in b.readers.items():
                if ridx != o.idx and rk != ekey:
                    deps.add(ridx)
            b.writer = o.idx
            b.readers = {ekey: o.idx} if wn in reads else {}
        o.deps = sorted(deps)
        return o.idx

    _claim_n = 0

    def claim_many(self, news, olds):
        deps = set()
        for on in olds:
            if on in self.bufs:
                b = self.bufs[on]
                if b.writer is not None:
                    deps.add(b.writer)
                deps.update(b.readers.values())
                b.writer = None; b.readers = {}
        for nn in news:
            b = self.buf(nn)
            for d in sorted(deps):
                Sched._claim_n += 1
                b.readers[("claim", Sched._claim_n)] = d

    def emit(self, nc, es):
        for o in self.ops:
            for d in o.deps:
                assert d < o.idx
                self.ops[d].need_inc = True
        esem = {e: es.enter_context(nc.semaphore(f"se_{e}")) for e in ENGINES}
        dsem = {s: es.enter_context(nc.semaphore(f"sd_{s}"))
                for s in self.dma_count}
        cnt = {e: 0 for e in ENGINES}; dcnt = {s: 0 for s in self.dma_count}
        for o in self.ops:
            if o.dma_sem is not None:
                dcnt[o.dma_sem] += 1
                o.inc_val = 16 * dcnt[o.dma_sem]
                o.need_inc = True
            elif o.need_inc:
                cnt[o.eng] += 1
                o.inc_val = cnt[o.eng]
        last_by_sem = {}
        latest_before = {}
        running = {}
        for o in self.ops:
            latest_before[o.idx] = dict(running)
            if o.dma_sem is not None:
                running[o.dma_sem] = o.idx
        for o in self.ops:
            nd = set()
            for d in o.deps:
                p = self.ops[d]
                if p.dma_sem is not None:
                    b = latest_before[o.idx].get(p.dma_sem)
                    assert b is not None and b >= d
                    nd.add(b)
                else:
                    nd.add(d)
            o.deps = sorted(nd)
        per_eng = {e: [] for e in ENGINES}
        for o in self.ops:
            per_eng[o.eng].append(o)
        opsref = self.ops
        blk = es.enter_context(nc.Block())
        for ename in ENGINES:
            olist = per_eng[ename]
            if not olist:
                continue
            section = getattr(blk, ename)

            def body(e, olist=olist):
                wm = {}
                for o in olist:
                    need = {}
                    for d in o.deps:
                        p = opsref[d]
                        key = ("d", p.dma_sem) if p.dma_sem is not None \
                            else ("e", p.eng)
                        if p.inc_val > need.get(key, -1):
                            need[key] = p.inc_val
                    for key, v in sorted(need.items()):
                        if v > wm.get(key, 0):
                            e.wait_ge(dsem[key[1]] if key[0] == "d"
                                      else esem[key[1]], v)
                            wm[key] = v
                    inst = o.fn(e)
                    if o.need_inc:
                        assert inst is not None, o.tag
                        if o.dma_sem is not None:
                            inst.then_inc(dsem[o.dma_sem], 16)
                        else:
                            inst.then_inc(esem[o.eng], 1)
            section(body)


def _blocked(w):
    O, I = w.shape
    n_oc, n_ic = O // 128, I // 128
    ocs = 32 // n_ic
    ns = n_oc // ocs
    v = w.reshape(ns, ocs, 128, n_ic, 128).transpose(0, 4, 1, 3, 2)
    return np.ascontiguousarray(v.reshape(ns, 128, 4096))


def _pcol(v):
    return np.ascontiguousarray(np.asarray(v, np.float32).reshape(-1, 128).T)


def build_program(debug=()):
    nc = bass.Bass("TRN2", target_bir_lowering=False)

    def din(name, shape, dtype=F32):
        return nc.dram_tensor(name, shape, dtype, kind="ExternalInput").ap()

    xo = din("xo", [D, TPC]); xp = din("xp", [D, TPC])
    flag = din("flag", [128, 1])
    wdr = {n: din(n, [s, 128, 4096]) for n, s in
           (("wq", 2), ("wk", 2), ("wv", 2), ("wo", 2), ("w1", 8), ("w2", 8),
            ("h1", 32), ("h2", 32), ("w3", 8))}
    proj2 = din("proj2", [128, 128])
    identc = din("identc", [128, 128])
    maskc = din("maskc", [128, 128])
    selc = din("selc", [2, 256])
    c16b = din("c16b", [128, 2])
    onesn = din("onesn", [128, 128])
    ones1 = din("ones1", [1, 128])
    bqc = din("bqc", [128, 8]); bkc = din("bkc", [128, 8]); boc = din("boc", [128, 8])
    bvrc = din("bvrc", [1, 1024])
    s1c = din("s1c", [128, 32]); s2c = din("s2c", [128, 32])
    hs1c = din("hs1c", [128, 32]); hs2c = din("hs2c", [128, 32])
    ws3c = din("ws3c", [128, 8])
    cstc = din("cstc", [128, 2])
    out = nc.dram_tensor("out", [D, TPC], F32, kind="ExternalOutput").ap()
    dbg = {}
    for name in debug:
        dbg[name] = nc.dram_tensor("dbg_" + name, list(TAP_SHAPES[name]), F32,
                                   kind="ExternalOutput").ap()

    es = ExitStack()
    sb = lambda n, s, d: es.enter_context(nc.sbuf_tensor(n, s, d))
    psum = lambda n, s, d: es.enter_context(nc.psum_tensor(n, s, d))

    xres = sb("xres", [128, NFC * TT], F32)
    wsl = [sb(f"wsl{i}", [128, 4096], BF16) for i in range(4)]
    S_ext = sb("S_ext", [128, H * 65], F32)
    pj2s = sb("pj2s", [128, 128], BF16)
    idn = sb("idn", [128, 128], BF16)
    msk = sb("msk", [128, 128], BF16)
    sel = sb("sel", [2, 256], BF16)
    c16p = sb("c16p", [128, 2], BF16)
    onn = sb("onn", [128, 128], BF16)
    on1 = sb("on1", [1, 128], BF16)
    bq_s = sb("bq_s", [128, 8], F32); bk_s = sb("bk_s", [128, 8], F32)
    bo_s = sb("bo_s", [128, 8], F32)
    s1_s = sb("s1_s", [128, 32], F32); s2_s = sb("s2_s", [128, 32], F32)
    hs1_s = sb("hs1_s", [128, 32], F32); hs2_s = sb("hs2_s", [128, 32], F32)
    ws3_s = sb("ws3_s", [128, 8], F32)
    flg = sb("flg", [128, 1], F32)
    cst = sb("cst", [128, 2], F32)
    pj2f = sb("pj2f", [128, 128], F32)
    mB = sb("mB", [128, 512], F32)
    vm = sb("vm", [128, 512], F32)
    t32 = sb("t32", [128, 512], BF16)
    t32h = sb("t32h", [128, 512], BF16)
    mBb = sb("mBb", [128, 512], BF16)
    vmb = sb("vmb", [128, 512], BF16)
    den = sb("den", [128, 16], F32)
    rec = sb("rec", [128, 16], F32)

    AR = 65536
    arena = sb("arena", [128, AR], BF16)
    OFF = {}
    def carve(name, n, off):
        OFF[name] = (off, n)
        assert off + n <= AR
        return off + n

    o = 0
    o = carve("xn", NFC * TT, o)
    o = carve("QK", 2 * NFC * TT, o)
    o = carve("vc", 8 * H * 65, o)
    o = carve("qkf", 2 * H * 256, o)
    o = carve("kt", 2 * H * 128, o)
    o = carve("ao", 4 * 1024, o)
    o = carve("af", NFC * 512, o)
    o = carve("ap_", NFC * 512, o)
    o = carve("sqs", 2048, o)
    o = carve("sq_sb", 2048, o)
    o = carve("S_bf", H * 65, o)
    o = carve("Asb", 256, o)
    o = carve("bv_s", 1024, o)
    assert o <= 65536 - 1648, f"arena used {o}"
    OFF["xb"] = (OFF["qkf"][0], 2 * NFC * 512)
    OFF["hb"] = (32768, 32768)
    OFF["g1"] = (0, 32768)
    OFF["yb"] = OFF["xn"]

    def A(name, p0=0, p1=128):
        off, n = OFF[name]
        return arena[p0:p1, off:off + n]

    def Asl(name, c0, c1, p0=0, p1=128):
        off, n = OFF[name]
        assert c1 <= n
        return arena[p0:p1, off + c0:off + c1]

    pb = [psum(f"pb{i}", [128, 512], F32) for i in range(7)]
    pT = psum("pT", [128, 1024], BF16)

    sch = Sched()
    A_KEYS = ["Q", "K", "QK", "qkf", "vc", "kt", "ao", "ap_", "sqs",
              "sq_sb", "S_bf", "Asb", "bv_s", "xb0", "xb1", "xn"] + [
              f"af{i}" for i in range(8)] + [
              f"ao{s}_{p}" for s in range(4) for p in range(8)]

    def gdma(dst, src, sem, reads=(), writes=(), tag=""):
        return sch.op("gpsimd", lambda e, d=dst, s=src: e.dma_start(d, s),
                      reads=reads, writes=writes, dma_sem=sem, tag=tag)

    def sdma(dst, src, sem, reads=(), writes=(), tag=""):
        return sch.op("sync", lambda e, d=dst, s=src: e.dma_start(d, s),
                      reads=reads, writes=writes, dma_sem=sem, tag=tag)

    _pool = [3, 4, 5, 6]
    _pool_i = [0]
    def pool():
        b = _pool[_pool_i[0] % 4]
        _pool_i[0] += 1
        return b

    class WStream:
        def __init__(self):
            self.items = []; self.emitted = 0
        def push(self, src, tag):
            self.items.append((src, tag))
            return len(self.items) - 1
        def prime(self, n=4):
            while self.emitted < min(n, len(self.items)):
                self._load(self.emitted)
        def _load(self, i):
            src, tag = self.items[i]
            slot = i % 4
            gdma(wsl[slot][:, :], src, f"dw{slot}", writes=[f"wsl{slot}"],
                 tag=f"ld_{tag}")
            self.emitted += 1
        def done_upto(self, i):
            while self.emitted < min(i + 5, len(self.items)):
                self._load(self.emitted)
        def slot(self, i):
            assert i < self.emitted
            return i % 4
    ws = WStream()

    def tile_ap(slab_i, j):
        s = ws.slot(slab_i)
        return wsl[s][:, j * 128:(j + 1) * 128], f"wsl{s}"

    gdma(idn[:, :], identc[:, :], "c0", writes=["idn"])
    gdma(msk[:, :], maskc[:, :], "c1", writes=["msk"])
    gdma(sel[:, :], selc[:, :], "c2", writes=["sel"])
    gdma(c16p[:, :], c16b[:, :], "c3", writes=["c16p"])
    gdma(onn[:, :], onesn[:, :], "c4", writes=["onn"])
    gdma(on1[:, :], ones1[:, :], "c5", writes=["on1"])
    sdma(bq_s[:, :], bqc[:, :], "c6", writes=["bq_s"])
    sdma(bk_s[:, :], bkc[:, :], "c7", writes=["bk_s"])
    sdma(bo_s[:, :], boc[:, :], "c8", writes=["bo_s"])
    sdma(s1_s[:, :], s1c[:, :], "c9", writes=["s1_s"])
    sdma(s2_s[:, :], s2c[:, :], "c10", writes=["s2_s"])
    sdma(hs1_s[:, :], hs1c[:, :], "c11", writes=["hs1_s"])
    sdma(hs2_s[:, :], hs2c[:, :], "c12", writes=["hs2_s"])
    sdma(ws3_s[:, :], ws3c[:, :], "c13", writes=["ws3_s"])
    sdma(flg[:, :], flag[:, :], "c14", writes=["flg"])
    sdma(cst[:, :], cstc[:, :], "c16", writes=["cst"])
    sdma(pj2f[:, :], proj2[:, :], "c15", writes=["pj2f"])
    sch.op("scalar", lambda e: e.activation(pj2s[:, :], pj2f[:, :], AF.Copy,
                                            scale=SCL_PROJ),
           reads=["pj2f"], writes=["pj2s"], tag="pjs")
    sch.op("vector", lambda e: e.memset(S_ext[:, :], 0.0), writes=["S"],
           tag="S0")

    def set_vc_ones():
        v3 = A("vc").rearrange("p (ch e) -> p ch e", e=65)
        sch.op("vector", lambda e, v3=v3: e.memset(v3[:, :, 64:65], 1.0),
               reads=["vc"], writes=["vc"], tag="vc1")

    def load_bv():
        gdma(Asl("bv_s", 0, 1024, 0, 1), bvrc[:, :], "dbv",
             reads=["bv_s"], writes=["bv_s"], tag="bv")

    def tap(name, src_ap, reads):
        if name not in dbg:
            return
        eng = gdma if str(src_ap.dtype) != "dt.float32" else sdma
        eng(dbg[name][:, :], src_ap, f"tp_{name}",
            reads=list(reads), writes=["tapdram"], tag=f"tap_{name}")
    sch.buf("tapdram")

    def ln_group(srcf, skey, xresf, dstf, dkey, mode, tagp, psplit=False,
                 xkey="xres"):
        b1, b2 = pool(), pool()
        for fc in range(NFC):
            sch.op("tensor",
                   lambda e, fc=fc: e.matmul(pb[b1][:, :], onn[:, :], srcf(fc),
                                             start=(fc == 0),
                                             stop=(fc == NFC - 1)),
                   reads=["onn", skey], writes=[f"pb{b1}"], tag=f"{tagp}s1")
        for half in range(2):
            for i in range(4):
                fc = half * 4 + i
                sch.op("vector",
                       lambda e, fc=fc, i=i: e.tensor_mul(
                           Asl("sqs", i * 512, (i + 1) * 512), srcf(fc),
                           srcf(fc)),
                       reads=[skey], writes=["sqs"], tag=f"{tagp}sq")
            for i in range(4):
                fc = half * 4 + i
                sch.op("tensor",
                       lambda e, fc=fc, i=i: e.matmul(
                           pb[b2][:, :], onn[:, :],
                           Asl("sqs", i * 512, (i + 1) * 512),
                           start=(fc == 0), stop=(fc == NFC - 1)),
                       reads=["onn", "sqs"], writes=[f"pb{b2}"],
                       tag=f"{tagp}s2")
        sch.op("vector", lambda e: e.tensor_copy(mB[:, :], pb[b1][:, :]),
               reads=[f"pb{b1}"], writes=["mB"], tag=f"{tagp}mB")
        sch.op("vector", lambda e: e.tensor_mul(vm[:, :], mB[:, :], mB[:, :]),
               reads=["mB"], writes=["vm"], tag=f"{tagp}m2")
        sch.op("vector", lambda e: e.scalar_tensor_tensor(
            vm[:, :], pb[b2][:, :], 1.0, vm[:, :], ALU.mult, ALU.subtract),
               reads=[f"pb{b2}", "vm"], writes=["vm"], tag=f"{tagp}var")
        sch.op("scalar", lambda e: e.activation(vm[:, :], vm[:, :], AF.Ln,
                                                bias=cst[:, 0:1]),
               reads=["vm", "cst"], writes=["vm"], tag=f"{tagp}ln")
        sch.op("scalar", lambda e: e.activation(vm[:, :], vm[:, :], AF.Exp,
                                                scale=-0.5),
               reads=["vm"], writes=["vm"], tag=f"{tagp}ex")
        sch.op("vector", lambda e: e.tensor_copy(mBb[:, :], mB[:, :]),
               reads=["mB"], writes=["mBb"], tag=f"{tagp}mbb")
        sch.op("vector", lambda e: e.tensor_copy(vmb[:, :], vm[:, :]),
               reads=["vm"], writes=["vmb"], tag=f"{tagp}vmb")
        for fc in range(NFC):
            sch.op("vector",
                   lambda e, fc=fc: e.tensor_sub(t32h[:, :], srcf(fc),
                                                 mBb[:, :]),
                   reads=[skey, "mBb"], writes=["t32h"], tag=f"{tagp}a1")
            if mode == "xn":
                sch.op("vector",
                       lambda e, fc=fc: e.tensor_mul(dstf(fc), t32h[:, :],
                                                     vmb[:, :]),
                       reads=["t32h", "vmb"], writes=[dkey], tag=f"{tagp}a2")
            else:
                sch.op("vector",
                       lambda e, fc=fc: e.tensor_mul(t32h[:, :], t32h[:, :],
                                                     vmb[:, :]),
                       reads=["t32h", "vmb"], writes=["t32h"],
                       tag=f"{tagp}a2")
                sch.op("vector",
                       lambda e, fc=fc: e.tensor_add(
                           xresf(fc), xresf(fc), t32h[:, :]),
                       reads=["t32h", xkey], writes=[xkey],
                       tag=f"{tagp}a3")

    def ln_full(srcf, skey, dstf, dkey, mode, tagp):
        for jg in range(2):
            ln_group(lambda fc, jg=jg: srcf(fc, jg), skey,
                     (lambda fc, jg=jg: xres[:, fc * TT + jg * 512:
                                             fc * TT + jg * 512 + 512]),
                     (lambda fc, jg=jg: dstf(fc, jg)) if dstf else None,
                     dkey, mode, f"{tagp}{jg}")

    def proj_qk(dkey, slab0, bias_sb, bkey, tagp, jgs=(0, 1)):
        for jg in jgs:
            for oc in range(NFC):
                si = slab0 + oc // 4
                bnk = pool()
                for ic in range(NFC):
                    j = (oc % 4) * 8 + ic
                    w_ap, wkey = tile_ap(si, j)
                    sch.op("tensor",
                           lambda e, w_ap=w_ap, ic=ic, jg=jg, bnk=bnk:
                           e.matmul(pb[bnk][:, :], w_ap,
                                    Asl("xn", ic * TT + jg * 512,
                                        ic * TT + jg * 512 + 512),
                                    start=(ic == 0), stop=(ic == NFC - 1)),
                           reads=[wkey, "xn"], writes=[f"pb{bnk}"],
                           tag=f"{tagp}mm")
                qk = 0 if dkey == "Q" else 1024
                sch.op("scalar",
                       lambda e, oc=oc, jg=jg, bnk=bnk, qk=qk: e.activation(
                           Asl("QK", oc * 2048 + qk + jg * 512,
                               oc * 2048 + qk + jg * 512 + 512),
                           pb[bnk][:, :], AF.Identity,
                           bias=bias_sb[:, oc:oc + 1]),
                       reads=[f"pb{bnk}", bkey], writes=[dkey],
                       tag=f"{tagp}ev")

    def v_upfront(slab0, tagp):
        for c in range(8):
            for og in range(2):
                si = slab0 + og
                bnk = pool()
                s = ws.slot(si)
                rhs = wsl[s].rearrange("p (o4 ic o) -> p o4 ic o", o4=4, ic=8)
                for ic in range(NFC):
                    sch.op("tensor",
                           lambda e, c=c, ic=ic, rhs=rhs, bnk=bnk: e.matmul(
                               pb[bnk][:, :],
                               Asl("xn", ic * TT + c * 128,
                                   ic * TT + (c + 1) * 128),
                               rhs[:, 0:4, ic, :],
                               start=(ic == 0), stop=False),
                           reads=[f"wsl{s}", "xn"], writes=[f"pb{bnk}"],
                           tag=f"{tagp}mm")
                sch.op("tensor",
                       lambda e, og=og, bnk=bnk: e.matmul(
                           pb[bnk][:, :], on1[0:1, :],
                           Asl("bv_s", og * 512, (og + 1) * 512, 0, 1),
                           start=False, stop=True),
                       reads=["on1", "bv_s"], writes=[f"pb{bnk}"],
                       tag=f"{tagp}b")
                v3 = A("vc").rearrange("p (ch e) -> p ch e", e=65)
                sch.op("scalar",
                       lambda e, c=c, og=og, bnk=bnk, v3=v3: e.activation(
                           v3[:, c * H + og * 8:c * H + (og + 1) * 8, 0:64],
                           pb[bnk][:, :], AF.Copy),
                       reads=[f"pb{bnk}"], writes=["vc"], tag=f"{tagp}ev")

    def phi_group(t0, with_q, tagp):
        def qk_ap(fc, r0, r1, qk, c0, c1):
            return Asl("QK", fc * 2048 + qk * 1024 + t0 + c0,
                       fc * 2048 + qk * 1024 + t0 + c1, r0, r1)

        def emit_sq(hp):
            so = (hp % 4) * 512
            if with_q:
                qk = A("QK").rearrange("p (fc a t) -> p fc a t",
                                       a=2, t=TT)[:, hp, :, t0:t0 + 256]
                sq2 = Asl("sqs", so, so + 512).rearrange(
                    "p (a t) -> p a t", a=2)
                sch.op("vector",
                       lambda e, qk=qk, sq2=sq2: e.tensor_mul(sq2, qk, qk),
                       reads=["Q", "K"], writes=["sqs"], tag=f"{tagp}qk2")
            else:
                sch.op("vector",
                       lambda e, hp=hp, so=so: e.tensor_mul(
                           Asl("sqs", so + 256, so + 512),
                           qk_ap(hp, 0, 128, 1, 0, 256),
                           qk_ap(hp, 0, 128, 1, 0, 256)),
                       reads=["K"], writes=["sqs"], tag=f"{tagp}k2")

        def emit_sqmm(hp):
            so = (hp % 4) * 512
            bnk = pool()
            sqc0 = 0 if with_q else 256
            sch.op("tensor",
                   lambda e, bnk=bnk, sqc0=sqc0, so=so: e.matmul(
                       pb[bnk][0:2, sqc0:512], c16p[:, :],
                       Asl("sqs", so + sqc0, so + 512),
                       start=True, stop=True),
                   reads=["c16p", "sqs"], writes=[f"pb{bnk}"],
                   tag=f"{tagp}sqm")
            sch.op("vector",
                   lambda e, bnk=bnk, sqc0=sqc0, so=so: e.tensor_copy(
                       Asl("sq_sb", so + sqc0, so + 512, 0, 2),
                       pb[bnk][0:2, sqc0:512]),
                   reads=[f"pb{bnk}"], writes=["sq_sb"], tag=f"{tagp}sqe")

        def emit_u(hp):
            bo = (hp % 4) * 512
            for hh in range(2):
                h = 2 * hp + hh
                base = 64 * hh
                selsl = sel[:, hh * 128:(hh + 1) * 128]
                if with_q:
                    rhs = A("QK").rearrange("p (fc a t) -> p fc a t",
                                            a=2, t=TT)[
                        base:base + 64, hp, :, t0:t0 + 256]
                    bnk = pool()
                    sch.op("tensor",
                           lambda e, rhs=rhs, base=base, bnk=bnk: e.matmul(
                               pb[bnk][:, 0:512],
                               pj2s[base:base + 64, :], rhs,
                               start=True, stop=False,
                               tile_position=(base, 0)),
                           reads=["pj2s", "Q", "K"], writes=[f"pb{bnk}"],
                           tag=f"{tagp}qku")
                    sch.op("tensor",
                           lambda e, selsl=selsl, bnk=bnk, bo=bo: e.matmul(
                               pb[bnk][:, 0:512], selsl,
                               Asl("sq_sb", bo, bo + 512, 0, 2),
                               start=False, stop=True),
                           reads=["sel", "sq_sb"], writes=[f"pb{bnk}"],
                           tag=f"{tagp}qkn")
                    sch.op("scalar",
                           lambda e, h=h, bnk=bnk: e.activation(
                               Asl("qkf", h * 512, (h + 1) * 512),
                               pb[bnk][:, 0:512], AF.Exp, bias=cst[:, 1:2]),
                           reads=[f"pb{bnk}", "cst"], writes=["qkf"],
                           tag=f"{tagp}qke")
                bnk = pool()
                for cc in range(2):
                    sch.op("tensor",
                           lambda e, hp=hp, base=base, cc=cc, bnk=bnk:
                           e.matmul(pb[bnk][:, cc * 128:(cc + 1) * 128],
                                    qk_ap(hp, base, base + 64, 1, cc * 128,
                                          (cc + 1) * 128),
                                    pj2s[base:base + 64, :],
                                    start=True, stop=False,
                                    tile_position=(base, 0)),
                           reads=["K", "pj2s"], writes=[f"pb{bnk}"],
                           tag=f"{tagp}tu")
                    sch.op("tensor",
                           lambda e, cc=cc, selsl=selsl, bnk=bnk, bo=bo:
                           e.matmul(pb[bnk][:, cc * 128:(cc + 1) * 128],
                                    Asl("sq_sb", bo + 256 + cc * 128,
                                        bo + 256 + (cc + 1) * 128, 0, 2),
                                    selsl, start=False, stop=True),
                           reads=["sq_sb", "sel"], writes=[f"pb{bnk}"],
                           tag=f"{tagp}tn")
                sch.op("scalar",
                       lambda e, h=h, bnk=bnk: e.activation(
                           Asl("kt", h * 256, (h + 1) * 256),
                           pb[bnk][:, 0:256], AF.Exp, bias=cst[:, 1:2]),
                       reads=[f"pb{bnk}", "cst"], writes=["kt"],
                       tag=f"{tagp}te")

        def eps_half(half):
            if with_q:
                sch.op("vector",
                       lambda e, half=half: e.tensor_scalar_add(
                           Asl("qkf", half * 4096, (half + 1) * 4096),
                           Asl("qkf", half * 4096, (half + 1) * 4096), PEPS),
                       reads=["qkf"], writes=["qkf"], tag=f"{tagp}ep")
            sch.op("vector",
                   lambda e, half=half: e.tensor_scalar_add(
                       Asl("kt", half * 2048, (half + 1) * 2048),
                       Asl("kt", half * 2048, (half + 1) * 2048), PEPS),
                   reads=["kt"], writes=["kt"], tag=f"{tagp}ept")

        for step in range(NFC + 3):
            if step < NFC:
                emit_sq(step)
            if 2 <= step < NFC + 2:
                emit_sqmm(step - 2)
            if step >= 3:
                emit_u(step - 3)
                if step == 6:
                    eps_half(0)
        eps_half(1)

    def s_bank(h):
        if h < 7: return 0, h * 65
        if h < 14: return 1, (h - 7) * 65
        return 2, (h - 14) * 65

    sbank_started = {}
    def s_start(bk):
        if sbank_started.get(bk):
            return False
        sbank_started[bk] = True
        return True

    def s_reset():
        sbank_started.clear()

    def s_carry_in(tagp):
        sch.op("vector", lambda e: e.tensor_copy(A("S_bf"), S_ext[:, :]),
               reads=["S"], writes=["S_bf"], tag=f"{tagp}cbf")
        s_reset()
        for h in range(H):
            bk, c0 = s_bank(h)
            st = s_start(bk)
            sch.op("tensor",
                   lambda e, h=h, bk=bk, c0=c0, st=st: e.matmul(
                       pb[bk][:, c0:c0 + 65], idn[:, :],
                       Asl("S_bf", h * 65, (h + 1) * 65),
                       start=st, stop=False),
                   reads=["idn", "S_bf"], writes=[f"pb{bk}"], tag=f"{tagp}ci")

    def s_carry_out(tagp):
        sch.op("vector", lambda e: e.tensor_copy(S_ext[:, 0:455],
                                                 pb[0][:, 0:455]),
               reads=["pb0"], writes=["S"], tag=f"{tagp}co0")
        sch.op("vector", lambda e: e.tensor_copy(S_ext[:, 455:910],
                                                 pb[1][:, 0:455]),
               reads=["pb1"], writes=["S"], tag=f"{tagp}co1")
        sch.op("vector", lambda e: e.tensor_copy(S_ext[:, 910:1040],
                                                 pb[2][:, 0:130]),
               reads=["pb2"], writes=["S"], tag=f"{tagp}co2")

    def s_cast(tagp):
        sch.op("scalar", lambda e: e.activation(Asl("S_bf", 0, 455),
                                                pb[0][:, 0:455], AF.Copy),
               reads=["pb0"], writes=["S_bf"], tag=f"{tagp}sc0")
        sch.op("scalar", lambda e: e.activation(Asl("S_bf", 455, 910),
                                                pb[1][:, 0:455], AF.Copy),
               reads=["pb1"], writes=["S_bf"], tag=f"{tagp}sc1")
        sch.op("scalar", lambda e: e.activation(Asl("S_bf", 910, 1040),
                                                pb[2][:, 0:130], AF.Copy),
               reads=["pb2"], writes=["S_bf"], tag=f"{tagp}sc2")

    def attn_chunk(c, cc, full, tagp):
        def emit_T1(h):
            qsl = Asl("qkf", h * 512 + cc * 128, h * 512 + (cc + 1) * 128)
            ksl = Asl("qkf", h * 512 + 256 + cc * 128,
                      h * 512 + 256 + (cc + 1) * 128)
            bA = 3 + (h % 2)
            sch.op("tensor",
                   lambda e, ksl=ksl, qsl=qsl, bA=bA: e.matmul(
                       pb[bA][:, 0:128], ksl, qsl, start=True, stop=True),
                   reads=["qkf"], writes=[f"pb{bA}"], tag=f"{tagp}T1")

        def emit_dv(h):
            aosl = Asl("ao", (c % 4) * 1024 + h * 64,
                       (c % 4) * 1024 + (h + 1) * 64)
            rk = "recL" if h < 8 else "recH"
            aok = f"ao{c % 4}_{h // 2}"
            sch.op("scalar",
                   lambda e, aosl=aosl, h=h: e.activation(
                       aosl, aosl, AF.Copy, scale=rec[:, h:h + 1]),
                   reads=[aok, rk], writes=[aok], tag=f"{tagp}dv")

        def emit_tr(fc):
            eng = "sync" if fc % 2 == 0 else "scalar"
            sch.op(eng,
                   lambda e, fc=fc: e.dma_start_transpose(
                       Asl("af", fc * 512 + (c % 4) * 128,
                           fc * 512 + (c % 4) * 128 + 128),
                       Asl("ao", (c % 4) * 1024 + fc * 128,
                           (c % 4) * 1024 + (fc + 1) * 128)),
                   reads=[f"ao{c % 4}_{fc}"], writes=[f"af{fc}"],
                   dma_sem=f"dtr{fc % 2}", tag=f"{tagp}tr")

        if full:
            emit_T1(0)
        for h in range(H):
            qsl = Asl("qkf", h * 512 + cc * 128, h * 512 + (cc + 1) * 128)
            ktsl = Asl("kt", h * 256 + cc * 128, h * 256 + (cc + 1) * 128)
            vsl = Asl("vc", (c * H + h) * 65, (c * H + h + 1) * 65)
            bk, c0 = s_bank(h)
            if full:
                if h + 1 < H:
                    emit_T1(h + 1)
                bA = 3 + (h % 2)
                asb = Asl("Asb", (h % 2) * 128, (h % 2) * 128 + 128)
                sch.op("vector",
                       lambda e, asb=asb, bA=bA: e.tensor_mul(
                           asb, pb[bA][:, 0:128], msk[:, :]),
                       reads=[f"pb{bA}", "msk"], writes=["Asb"],
                       tag=f"{tagp}mk")
            st = s_start(bk)
            sch.op("tensor",
                   lambda e, ktsl=ktsl, vsl=vsl, bk=bk, c0=c0, st=st:
                   e.matmul(pb[bk][:, c0:c0 + 65], ktsl, vsl,
                            start=st, stop=False),
                   reads=["kt", "vc"], writes=[f"pb{bk}"], tag=f"{tagp}T4")
            if full:
                bN = 5 + (h % 2)
                sch.op("tensor",
                       lambda e, asb=asb, vsl=vsl, bN=bN: e.matmul(
                           pb[bN][:, 0:65], asb, vsl, start=True, stop=False),
                       reads=["Asb", "vc"], writes=[f"pb{bN}"],
                       tag=f"{tagp}T2")
                sch.op("tensor",
                       lambda e, qsl=qsl, h=h, bN=bN: e.matmul(
                           pb[bN][:, 0:65], qsl,
                           Asl("S_bf", h * 65, (h + 1) * 65),
                           start=False, stop=True),
                       reads=["qkf", "S_bf"], writes=[f"pb{bN}"],
                       tag=f"{tagp}T3")
                dcol = den[:, h:h + 1]
                sch.op("vector",
                       lambda e, dcol=dcol, bN=bN: e.tensor_scalar_add(
                           dcol, pb[bN][:, 64:65], AEPS),
                       reads=[f"pb{bN}"], writes=["den"], tag=f"{tagp}d1")
                aosl = Asl("ao", (c % 4) * 1024 + h * 64,
                           (c % 4) * 1024 + (h + 1) * 64)
                aok = f"ao{c % 4}_{h // 2}"
                sch.op("vector",
                       lambda e, aosl=aosl, bN=bN: e.tensor_copy(
                           aosl, pb[bN][:, 0:64]),
                       reads=[f"pb{bN}"], writes=[aok], tag=f"{tagp}nc")
                if h == 7:
                    sch.op("vector",
                           lambda e: e.reciprocal(rec[:, 0:8], den[:, 0:8]),
                           reads=["den"], writes=["recL"], tag=f"{tagp}rcL")
                if h >= 8:
                    emit_dv(h - 8)
                    if h % 2 == 1:
                        emit_tr((h - 9) // 2)
        if full:
            sch.op("vector",
                   lambda e: e.reciprocal(rec[:, 8:16], den[:, 8:16]),
                   reads=["den"], writes=["recH"], tag=f"{tagp}rcH")
            s_cast(tagp)
            for j in range(4):
                emit_dv(8 + 2 * j)
                emit_dv(9 + 2 * j)
                emit_tr(4 + j)

    def wo_proj(g, slab0, tagp):
        for oc in range(NFC):
            si = slab0 + oc // 4
            bnk = pool()
            for ic in range(NFC):
                j = (oc % 4) * 8 + ic
                w_ap, wkey = tile_ap(si, j)
                sch.op("tensor",
                       lambda e, w_ap=w_ap, ic=ic, bnk=bnk: e.matmul(
                           pb[bnk][:, :], w_ap,
                           Asl("af", ic * 512, (ic + 1) * 512),
                           start=(ic == 0), stop=(ic == NFC - 1)),
                       reads=[wkey, f"af{ic}"], writes=[f"pb{bnk}"],
                       tag=f"{tagp}mm")
            sch.op("scalar",
                   lambda e, oc=oc, bnk=bnk: e.activation(
                       Asl("ap_", oc * 512, (oc + 1) * 512), pb[bnk][:, :],
                       AF.Identity, bias=bo_s[:, oc:oc + 1]),
                   reads=[f"pb{bnk}", "bo_s"], writes=["ap_"],
                   tag=f"{tagp}ev")

    def ln3(g, tagp):
        ln_group(lambda fc: Asl("ap_", fc * 512, (fc + 1) * 512), "ap_",
                 lambda fc: xres[:, fc * TT + g * 512:
                                 fc * TT + (g + 1) * 512],
                 None, None, "res", f"{tagp}ln", xkey=f"xres{g}")

    def xsrc(src, t):
        return src.rearrange("(fc p) t -> p fc t", p=128)[
            :, :, t * TT:(t + 1) * TT]

    pfx = [ws.push(wdr["wk"][i], f"pwk{i}") for i in range(2)] + \
          [ws.push(wdr["wv"][i], f"pwv{i}") for i in range(2)]
    tile_slabs = []
    for t in range(NT):
        d = {}
        d["wq"] = [ws.push(wdr["wq"][i], f"t{t}wq{i}") for i in range(2)]
        d["wk"] = [ws.push(wdr["wk"][i], f"t{t}wk{i}") for i in range(2)]
        d["wv"] = [ws.push(wdr["wv"][i], f"t{t}wv{i}") for i in range(2)]
        d["wo"] = [ws.push(wdr["wo"][i], f"t{t}wo{i}") for i in range(2)]
        s1l = []
        for b in range(8):
            s1l.append(ws.push(wdr["w1"][b], f"t{t}w1_{b}"))
            s1l.append(ws.push(wdr["w2"][b], f"t{t}w2_{b}"))
        d["s1"] = s1l
        d["h1"] = [ws.push(wdr["h1"][i], f"t{t}h1_{i}") for i in range(32)]
        d["h2"] = [ws.push(wdr["h2"][i], f"t{t}h2_{i}") for i in range(32)]
        d["w3"] = [ws.push(wdr["w3"][i], f"t{t}w3_{i}") for i in range(8)]
        tile_slabs.append(d)
    ws.prime(4)

    set_vc_ones()
    load_bv()

    s_reset()
    for pt in range(2):
        tg = f"p{pt}"
        for jg in range(2):
            sdma(xres.rearrange("p (fc t) -> p fc t", t=TT)[
                     :, :, jg * 512:(jg + 1) * 512],
                 xsrc(xp, pt)[:, :, jg * 512:(jg + 1) * 512], f"dx{jg}",
                 reads=[f"xres{jg}"], writes=[f"xres{jg}"],
                 tag=f"{tg}x{jg}")
            gdma(A("xb").rearrange("p (fc t) -> p fc t", t=TT)[
                     :, :, jg * 512:(jg + 1) * 512],
                 xsrc(xp, pt)[:, :, jg * 512:(jg + 1) * 512], f"dxb{jg}",
                 reads=[f"xb{jg}"], writes=[f"xb{jg}"], tag=f"{tg}xb{jg}")
        for jg in range(2):
            ln_group(lambda fc, jg=jg: Asl("xb", fc * TT + jg * 512,
                                           fc * TT + jg * 512 + 512),
                     f"xb{jg}",
                     lambda fc, jg=jg: xres[:, fc * TT + jg * 512:
                                            fc * TT + jg * 512 + 512],
                     lambda fc, jg=jg: Asl("xn", fc * TT + jg * 512,
                                           fc * TT + jg * 512 + 512),
                     "xn", "xn", f"{tg}l1_{jg}", psplit=True,
                     xkey=f"xres{jg}")
            proj_qk("K", pfx[0], bk_s, "bk_s", f"{tg}K", jgs=(jg,))
        if pt == 0:
            tap("pxn0", A("xn"), ["xn"])
        if pt == 1:
            ws.done_upto(pfx[1])
        v_upfront(pfx[2], f"{tg}V")
        if pt == 1:
            ws.done_upto(pfx[3])
        if pt == 0:
            tap("pvc0", A("vc"), ["vc"])
        for g in range(4):
            phi_group(g * 256, False, f"{tg}f{g}")
            if pt == 0 and g == 0:
                tap("pkt0", A("kt"), ["kt"])
            for cc in range(2):
                attn_chunk(g * 2 + cc, cc, False, f"{tg}c{g * 2 + cc}")
    s_carry_out("pco")
    sch.op("vector", lambda e: e.tensor_scalar_mul(S_ext[:, :], S_ext[:, :],
                                                   flg[:, :]),
           reads=["S", "flg"], writes=["S"], tag="Sflag")
    tap("Spfx", S_ext[:, :], ["S"])

    for t in range(NT):
        tg = f"t{t}"
        sl = tile_slabs[t]
        sch.claim_many(A_KEYS, ["hb", "g1", "yb"])
        if t > 0:
            set_vc_ones()
            load_bv()
        for jg in range(2):
            sdma(xres.rearrange("p (fc t) -> p fc t", t=TT)[
                     :, :, jg * 512:(jg + 1) * 512],
                 xsrc(xo, t)[:, :, jg * 512:(jg + 1) * 512], f"dx{jg}",
                 reads=[f"xres{jg}"], writes=[f"xres{jg}"],
                 tag=f"{tg}x{jg}")
            gdma(A("xb").rearrange("p (fc t) -> p fc t", t=TT)[
                     :, :, jg * 512:(jg + 1) * 512],
                 xsrc(xo, t)[:, :, jg * 512:(jg + 1) * 512], f"dxb{jg}",
                 reads=[f"xb{jg}"], writes=[f"xb{jg}"], tag=f"{tg}xb{jg}")
        for jg in range(2):
            ln_group(lambda fc, jg=jg: Asl("xb", fc * TT + jg * 512,
                                           fc * TT + jg * 512 + 512),
                     f"xb{jg}",
                     lambda fc, jg=jg: xres[:, fc * TT + jg * 512:
                                            fc * TT + jg * 512 + 512],
                     lambda fc, jg=jg: Asl("xn", fc * TT + jg * 512,
                                           fc * TT + jg * 512 + 512),
                     "xn", "xn", f"{tg}l1_{jg}", psplit=True,
                     xkey=f"xres{jg}")
            proj_qk("Q", sl["wq"][0], bq_s, "bq_s", f"{tg}Q", jgs=(jg,))
        if t == 0:
            tap("xn0", A("xn"), ["xn"])
        ws.done_upto(sl["wq"][1])
        proj_qk("K", sl["wk"][0], bk_s, "bk_s", f"{tg}K")
        ws.done_upto(sl["wk"][1])

        v_upfront(sl["wv"][0], f"{tg}V")
        ws.done_upto(sl["wv"][1])
        if t == 0:
            tap("vc0", A("vc"), ["vc"])
        s_carry_in(f"{tg}ci")
        pend_ln3 = None
        for g in range(4):
            phi_group(g * 256, True, f"{tg}f{g}")
            if pend_ln3 is not None:
                pend_ln3(); pend_ln3 = None
            if t == 0 and g == 0:
                tap("kt0", A("kt"), ["kt"])
            for cc in range(2):
                c = g * 2 + cc
                attn_chunk(c, cc, True, f"{tg}c{c}")
            if g % 2 == 1:
                if t == 0 and g == 1:
                    tap("ao0", A("ao"), [f"ao{s}_{p}" for s in range(4) for p in range(8)])
                    tap("af0", A("af"), [f"af{i}" for i in range(8)])
                wo_proj(g // 2, sl["wo"][0], f"{tg}wo{g // 2}")
                pend_ln3 = (lambda gg=g // 2: ln3(gg, f"{tg}wo{gg}"))
                if t == 0 and g == 1:
                    tap("ap0", A("ap_"), ["ap_"])
        if pend_ln3 is not None:
            pend_ln3(); pend_ln3 = None
        ws.done_upto(sl["wo"][1])
        s_carry_out(f"{tg}co")
        if t == 0:
            tap("x2_0", xres[:, :], ["xres0", "xres1"])
            tap("S_t0", S_ext[:, :], ["S"])

        sch.claim_many(["hb", "g1"],
                       ["Q", "K", "QK", "qkf", "vc", "kt", "ao", "ap_",
                        "S_bf", "Asb", "bv_s"] + [
                        f"af{i}" for i in range(8)] + [
                        f"ao{s}_{p}" for s in range(4) for p in range(8)])
        for jg in range(2):
            for fc in range(NFC):
                sch.op("vector",
                       lambda e, fc=fc, jg=jg: e.tensor_copy(
                           Asl("xb", fc * TT + jg * 512,
                               fc * TT + jg * 512 + 512),
                           xres[:, fc * TT + jg * 512:
                                fc * TT + jg * 512 + 512]),
                       reads=[f"xres{jg}"], writes=[f"xb{jg}"],
                       tag=f"{tg}l2c")
            ln_group(lambda fc, jg=jg: Asl("xb", fc * TT + jg * 512,
                                           fc * TT + jg * 512 + 512),
                     f"xb{jg}",
                     lambda fc, jg=jg: xres[:, fc * TT + jg * 512:
                                            fc * TT + jg * 512 + 512],
                     lambda fc, jg=jg: Asl("xn", fc * TT + jg * 512,
                                           fc * TT + jg * 512 + 512),
                     "xn", "xn", f"{tg}l2_{jg}", psplit=True,
                     xkey=f"xres{jg}")
        if t == 0:
            tap("xn2_0", A("xn"), ["xn"])
        sch.claim_many(["hb"], ["xb0", "xb1", "sqs", "sq_sb"])
        for b in range(8):
            i1, i2 = sl["s1"][2 * b], sl["s1"][2 * b + 1]
            for o4 in range(4):
                oc = b * 4 + o4
                for jg in range(2):
                    b1, b2 = pool(), pool()
                    for ic in range(NFC):
                        j = o4 * 8 + ic
                        w1ap, w1k = tile_ap(i1, j)
                        w2ap, w2k = tile_ap(i2, j)
                        sch.op("tensor",
                               lambda e, w1ap=w1ap, ic=ic, jg=jg, b1=b1:
                               e.matmul(pb[b1][:, :], w1ap,
                                        Asl("xn", ic * TT + jg * 512,
                                            ic * TT + jg * 512 + 512),
                                        start=(ic == 0),
                                        stop=(ic == NFC - 1)),
                               reads=[w1k, "xn"], writes=[f"pb{b1}"],
                               tag=f"{tg}s1a")
                        sch.op("tensor",
                               lambda e, w2ap=w2ap, ic=ic, jg=jg, b2=b2:
                               e.matmul(pb[b2][:, :], w2ap,
                                        Asl("xn", ic * TT + jg * 512,
                                            ic * TT + jg * 512 + 512),
                                        start=(ic == 0),
                                        stop=(ic == NFC - 1)),
                               reads=[w2k, "xn"], writes=[f"pb{b2}"],
                               tag=f"{tg}s1b")
                    sch.op("scalar",
                           lambda e, oc=oc, b1=b1: e.activation(
                               t32[:, :], pb[b1][:, :], AF.Silu,
                               scale=s1_s[:, oc:oc + 1]),
                           reads=[f"pb{b1}", "s1_s"], writes=["t32"],
                           tag=f"{tg}si")
                    sch.op("vector",
                           lambda e, oc=oc, jg=jg, b2=b2:
                           e.scalar_tensor_tensor(
                               Asl("hb", oc * TT + jg * 512,
                                   oc * TT + jg * 512 + 512),
                               pb[b2][:, :], s2_s[:, oc:oc + 1], t32[:, :],
                               ALU.mult, ALU.mult),
                           reads=[f"pb{b2}", "s2_s", "t32"], writes=["hb"],
                           tag=f"{tg}hm")
            ws.done_upto(i2)
        if t == 0:
            tap("h0", A("hb"), ["hb"])
        sch.claim_many(["g1"], ["xn"])

        def hstage(slabs, skey, src_name, dkey, dst_name, scol, sckey, tagp):
            for oc in range(NHC):
                si = slabs[oc]
                for jg in range(2):
                    bnk = pool()
                    for ic in range(NHC):
                        w_ap, wkey = tile_ap(si, ic)
                        sch.op("tensor",
                               lambda e, w_ap=w_ap, ic=ic, jg=jg, bnk=bnk:
                               e.matmul(pb[bnk][:, :], w_ap,
                                        Asl(src_name, ic * TT + jg * 512,
                                            ic * TT + jg * 512 + 512),
                                        start=(ic == 0),
                                        stop=(ic == NHC - 1)),
                               reads=[wkey, skey], writes=[f"pb{bnk}"],
                               tag=f"{tagp}mm")
                    sch.op("scalar",
                           lambda e, oc=oc, jg=jg, bnk=bnk: e.activation(
                               Asl(dst_name, oc * TT + jg * 512,
                                   oc * TT + jg * 512 + 512),
                               pb[bnk][:, :], AF.Gelu,
                               scale=scol[:, oc:oc + 1]),
                           reads=[f"pb{bnk}", sckey], writes=[dkey],
                           tag=f"{tagp}ev")
                ws.done_upto(si)
        hstage(sl["h1"], "hb", "hb", "g1", "g1", hs1_s, "hs1_s", f"{tg}H1")
        if t == 0:
            tap("g10", A("g1"), ["g1"])
        hstage(sl["h2"], "g1", "g1", "hb", "hb", hs2_s, "hs2_s", f"{tg}H2")
        if t == 0:
            tap("g20", A("hb"), ["hb"])
        sch.claim_many(["yb"], ["g1"])
        for oc in range(NFC):
            si = sl["w3"][oc]
            for jg in range(2):
                bnk = pool()
                for ic in range(NHC):
                    w_ap, wkey = tile_ap(si, ic)
                    sch.op("tensor",
                           lambda e, w_ap=w_ap, ic=ic, jg=jg, bnk=bnk:
                           e.matmul(pb[bnk][:, :], w_ap,
                                    Asl("hb", ic * TT + jg * 512,
                                        ic * TT + jg * 512 + 512),
                                    start=(ic == 0), stop=(ic == NHC - 1)),
                           reads=[wkey, "hb"], writes=[f"pb{bnk}"],
                           tag=f"{tg}w3mm")
                sch.op("vector",
                       lambda e, oc=oc, jg=jg, bnk=bnk: e.tensor_scalar_mul(
                           Asl("yb", oc * TT + jg * 512,
                               oc * TT + jg * 512 + 512),
                           pb[bnk][:, :], ws3_s[:, oc:oc + 1]),
                       reads=[f"pb{bnk}", "ws3_s"], writes=["yb"],
                       tag=f"{tg}w3ev")
            ws.done_upto(si)
        if t == 0:
            tap("y0", A("yb"), ["yb"])
        sch.claim_many(["sqs", "hb"], ["hb"])
        for jg in range(2):
            ln_group(lambda fc, jg=jg: Asl("yb", fc * TT + jg * 512,
                                           fc * TT + jg * 512 + 512), "yb",
                     lambda fc, jg=jg: xres[:, fc * TT + jg * 512:
                                            fc * TT + jg * 512 + 512],
                     None, None, "res", f"{tg}l4_{jg}", psplit=True,
                     xkey=f"xres{jg}")
        for jg in range(2):
            sdma(out.rearrange("(fc p) t -> p fc t", p=128)[
                     :, :, t * TT + jg * 512:t * TT + (jg + 1) * 512],
                 xres.rearrange("p (fc t) -> p fc t", t=TT)[
                     :, :, jg * 512:(jg + 1) * 512], f"dout{jg}",
                 reads=[f"xres{jg}"], writes=["outdram"], tag=f"{tg}out{jg}")

    sch.op("sync", lambda e: None, reads=["outdram", "tapdram"], tag="fin")
    sch.emit(nc, es)
    es.close()
    return nc


_prog_cache = {}

def _get_program():
    if "p" not in _prog_cache:
        _prog_cache["p"] = build_program(debug=_DEBUG)
    return _prog_cache["p"]


def kernel(L0_attn_x, L0_attn_Wq, L0_attn_bq, L0_attn_Wk, L0_attn_bk,
           L0_attn_Wv, L0_attn_bv, L0_attn_Wo, L0_attn_bo, L0_attn_proj,
           L0_ffn_w1_w, L0_ffn_w1_s, L0_ffn_w2_w, L0_ffn_w2_s,
           L0_ffn_h1_w, L0_ffn_h1_s, L0_ffn_h2_w, L0_ffn_h2_s,
           L0_ffn_w3_w, L0_ffn_w3_s):
    f32 = lambda a: np.ascontiguousarray(np.asarray(a, dtype=np.float32))
    x = f32(L0_attn_x)
    pj = f32(L0_attn_proj)
    selv = np.zeros((2, 256), np.float32)
    selv[0, 0:128] = -1.0; selv[1, 128:256] = -1.0
    c16bv = np.zeros((128, 2), np.float32)
    c16bv[0:64, 0] = C16; c16bv[64:128, 1] = C16
    shared = {
        "wq": _blocked(f32(L0_attn_Wq)), "wk": _blocked(f32(L0_attn_Wk)),
        "wv": _blocked(f32(L0_attn_Wv)), "wo": _blocked(f32(L0_attn_Wo)),
        "w1": _blocked(f32(L0_ffn_w1_w)), "w2": _blocked(f32(L0_ffn_w2_w)),
        "h1": _blocked(f32(L0_ffn_h1_w)), "h2": _blocked(f32(L0_ffn_h2_w)),
        "w3": _blocked(f32(L0_ffn_w3_w)),
        "proj2": np.ascontiguousarray(np.concatenate([pj.T, pj.T], axis=0)),
        "identc": np.eye(128, dtype=np.float32),
        "maskc": np.ascontiguousarray(np.triu(np.ones((128, 128),
                                                      np.float32))),
        "selc": selv, "c16b": c16bv,
        "onesn": np.full((128, 128), INV_D, np.float32),
        "ones1": np.ones((1, 128), np.float32),
        "bqc": _pcol(L0_attn_bq), "bkc": _pcol(L0_attn_bk),
        "boc": _pcol(L0_attn_bo),
        "bvrc": f32(L0_attn_bv).reshape(1, 1024),
        "s1c": _pcol(L0_ffn_w1_s), "s2c": _pcol(L0_ffn_w2_s),
        "hs1c": _pcol(L0_ffn_h1_s), "hs2c": _pcol(L0_ffn_h2_s),
        "ws3c": _pcol(L0_ffn_w3_s),
        "cstc": np.ascontiguousarray(np.broadcast_to(
            np.array([LNEPS, EXP_BIAS], np.float32), (128, 2))),
    }
    in_maps = []
    for c in range(8):
        b, half = c // 2, c % 2
        m = dict(shared)
        m["xo"] = np.ascontiguousarray(x[b, half * TPC:(half + 1) * TPC, :].T)
        if half == 0:
            m["xp"] = np.zeros((D, TPC), np.float32)
            m["flag"] = np.zeros((128, 1), np.float32)
        else:
            m["xp"] = np.ascontiguousarray(x[b, 0:TPC, :].T)
            m["flag"] = np.ones((128, 1), np.float32)
        in_maps.append(m)
    nc = _get_program()
    res = run_bass_kernel_spmd(nc, in_maps, core_ids=list(range(8)))
    globals()["_last_results"] = res.results
    outp = np.empty((4, 4096, 1024), np.float32)
    for c in range(8):
        b, half = c // 2, c % 2
        outp[b, half * TPC:(half + 1) * TPC, :] = res.results[c]["out"].T
    return outp
